# version 05
# baseline (speedup 1.0000x reference)
import math
import numpy as np
import ml_dtypes
import concourse.bass as bass
import concourse.mybir as mybir
from concourse.bass_utils import run_bass_kernel_spmd

F32 = mybir.dt.float32
BF16 = mybir.dt.bfloat16
AF = mybir.ActivationFunctionType
ALU = mybir.AluOpType
AX = mybir.AxisListType

NCORES = 8
D = 2048
T = 2048
NCH = 16
NTS = 16
NTB = 4
DENOM = math.sqrt(2048.0) / 4096.0
EPS = 1e-5

ENGMAP = {"pe": "tensor", "dve": "vector", "act": "scalar", "gp": "gpsimd", "sp": "sync"}


class Prog:

    def __init__(self):
        self.ops = []

    def op(self, eng, fn, deps=(), kind="c", seg=0):
        i = len(self.ops)
        self.ops.append({
            "eng": eng, "fn": fn, "kind": kind, "seg": seg,
            "deps": [d for d in deps if d is not None],
        })
        return i

    def dma(self, eng, fn, deps=(), seg=0):
        return self.op(eng, fn, deps, kind="d", seg=seg)

    def emit(self, nc, block, sems):
        ops = self.ops
        def semkey(o):
            if o["kind"] == "cc":
                return "cc"
            if o["kind"] == "d":
                return o["eng"] + "q"
            return o["eng"]
        targets = set()
        for o in ops:
            for d in o["deps"]:
                targets.add(d)
        counts = {}
        for i, o in enumerate(ops):
            o["inc"] = 0
            if i in targets or o["kind"] in ("d", "cc"):
                k = semkey(o)
                step = 16 if o["kind"] == "d" else 1
                counts[k] = counts.get(k, 0) + step
                o["inc"] = step
                o["semk"] = k
                o["val"] = counts[k]
        nsegs = 1 + max(o["seg"] for o in ops)
        for seg in range(nsegs):
            for eng in ("gp", "sp", "act", "dve", "pe"):
                mine = [o for o in ops if o["seg"] == seg and o["eng"] == eng]
                if not mine:
                    continue
                def body(e, mine=mine, eng=eng):
                    marks = {}
                    for o in mine:
                        for d in o["deps"]:
                            p = ops[d]
                            if p["inc"] == 0:
                                raise RuntimeError("dep on non-target op")
                            k, v = p["semk"], p["val"]
                            if marks.get(k, -1) >= v:
                                continue
                            e.wait_ge(sems[k], v)
                            marks[k] = v
                        inst = o["fn"](e)
                        if o["inc"]:
                            inst.then_inc(sems[o["semk"]], o["inc"])
                getattr(block, ENGMAP[eng])(body)


def build_nc():
    nc = bass.Bass("TRN2", target_bir_lowering=False)

    d_xtok = nc.dram_tensor("xtok", [T, D], F32, kind="ExternalInput").ap()
    d_xfeat = nc.dram_tensor("xfeat", [D, T], F32, kind="ExternalInput").ap()
    d_wqT = nc.dram_tensor("wqT", [D, D], F32, kind="ExternalInput").ap()
    d_wkT = nc.dram_tensor("wkT", [D, D], F32, kind="ExternalInput").ap()
    d_wvT = nc.dram_tensor("wvT", [D, D], F32, kind="ExternalInput").ap()
    d_woT = nc.dram_tensor("woT", [D, D], F32, kind="ExternalInput").ap()
    d_w3T = nc.dram_tensor("w3T", [D, D], F32, kind="ExternalInput").ap()
    d_sw = nc.dram_tensor("sw", [8, T * D], F32, kind="ExternalInput").ap()
    d_swb = nc.dram_tensor("swb", [1, 8], F32, kind="ExternalInput").ap()
    d_w1 = nc.dram_tensor("w1", [8 * 128, 128], F32, kind="ExternalInput").ap()
    d_w2 = nc.dram_tensor("w2", [8 * 128, 128], F32, kind="ExternalInput").ap()
    d_b1 = nc.dram_tensor("b1", [8, 128], F32, kind="ExternalInput").ap()
    d_b2 = nc.dram_tensor("b2", [8, 128], F32, kind="ExternalInput").ap()
    d_b3 = nc.dram_tensor("b3", [1, D], F32, kind="ExternalInput").ap()
    d_sel1 = nc.dram_tensor("sel1", [8, 1], F32, kind="ExternalInput").ap()
    d_sel2 = nc.dram_tensor("sel2", [8, 1], F32, kind="ExternalInput").ap()
    d_out = nc.dram_tensor("out", [T, D], F32, kind="ExternalOutput").ap()

    d_x1f = nc.dram_tensor("x1f", [T, D], F32, kind="Internal").ap()
    d_x1b = nc.dram_tensor("x1b", [T, D], BF16, kind="Internal").ap()
    d_wfd = nc.dram_tensor("wfd", [4 * 128, 128], BF16, kind="Internal").ap()
    d_ci1 = nc.dram_tensor("ci1", [1, D], F32, kind="Internal").ap()
    d_co1 = nc.dram_tensor("co1", [8, D], F32, kind="Internal").ap()
    d_ci2 = nc.dram_tensor("ci2", [1, 8], F32, kind="Internal").ap()
    d_co2 = nc.dram_tensor("co2", [8, 8], F32, kind="Internal").ap()

    ctx = []
    def sb(name, shape, dt):
        h = nc.sbuf_tensor(name, shape, dt)
        t = h.__enter__(); ctx.append(h); return t
    def psum(name, shape, dt=F32):
        h = nc.psum_tensor(name, shape, dt)
        t = h.__enter__(); ctx.append(h); return t

    bufA = sb("bufA", [128, NCH * T], BF16)
    bufB = sb("bufB", [128, NCH * T], BF16)
    wk2 = sb("wk2", [128, 2 * D], BF16)
    wv2 = sb("wv2", [128, 2 * D], BF16)
    sww = sb("sww", [128, 2 * D], BF16)
    qh = sb("qh", [128, 2 * T], BF16)
    x1b2 = sb("x1b2", [128, 2 * T], BF16)
    wexp = sb("wexp", [128, 4 * 128], BF16)
    w1all = sb("w1all", [128, 8 * 128], BF16)
    w2all = sb("w2all", [128, 8 * 128], BF16)
    w1fd = sb("w1fd", [128, 4 * 128], BF16)

    xw = sb("xw", [128, T], F32)
    scr = sb("scr", [128, T], F32)
    x1t = sb("x1t", [128, T], F32)
    b3row = sb("b3row", [1, D], BF16)
    onesb3 = sb("onesb3", [1, 128], BF16)
    identc = sb("identc", [128, 1], F32)
    ccols = sb("ccols", [128, 16], F32)
    ag2s = sb("ag2s", [8, 8], F32)
    selT = sb("selT", [8, 2], F32)
    selbc = sb("selbc", [128, 16], F32)
    b1sb = sb("b1sb", [8, 128], F32)
    b2sb = sb("b2sb", [8, 128], F32)
    b1cols = sb("b1cols", [128, 2], F32)
    b2cols = sb("b2cols", [128, 2], F32)
    bc2col = sb("bc2col", [128, 1], F32)
    gbc = sb("gbc", [128, 2], F32)
    logacc = sb("logacc", [128, 8], F32)
    cols = sb("cols", [128, 16], F32)
    C_DEN, C_ZERO, C_EPS, C_1OD, C_M, C_RSTD, C_S2, C_VAR, C_MS, C_STD, C_M2, C_S2M, C_RA, C_RB, C_ONE = range(15)
    rows = sb("rows", [1, 64], F32)
    R_LG, R_L2, R_EXP, R_PR, R_OH1, R_OH2, R_PM, R_T1 = range(8)
    s11 = sb("s11", [1, 8], F32)
    S_NEGMAX, S_SUM, S_RSUM, S_M1, S_M2 = range(5)
    onesrow = sb("onesrow", [1, 128], F32)
    swbrow = sb("swbrow", [1, 8], F32)
    sel1c = sb("sel1c", [8, 1], F32)
    sel2c = sb("sel2c", [8, 1], F32)

    pk = psum("pk", [128, T], F32)
    pv = psum("pv", [128, T], F32)

    sems = {}
    for name in ("pe", "dve", "act", "gpq", "spq", "actq", "cc"):
        h = nc.semaphore(name)
        sems[name] = h.__enter__(); ctx.append(h)

    block_h = nc.Block(); block = block_h.__enter__(); ctx.append(block_h)

    P = Prog()
    def col(i):
        return cols[:, i:i + 1]
    def row(i):
        return rows[:, 8 * i:8 * i + 8]
    def sc(i):
        return s11[:, i:i + 1]

    e_ident = P.op("dve", lambda e: e.memset(identc[:], 1.0))
    e_b3 = P.dma("gp", lambda e: e.dma_start(out=b3row[:], in_=d_b3))
    e_ob3 = P.op("dve", lambda e: e.memset(onesb3[:], 1.0))
    e_swb = P.dma("sp", lambda e: e.dma_start(out=swbrow[:], in_=d_swb))
    e_sel1 = P.dma("sp", lambda e: e.dma_start(out=sel1c[:], in_=d_sel1))
    e_sel2 = P.dma("sp", lambda e: e.dma_start(out=sel2c[:], in_=d_sel2))
    e_b1 = P.dma("sp", lambda e: e.dma_start(out=b1sb[:], in_=d_b1))
    e_b2 = P.dma("sp", lambda e: e.dma_start(out=b2sb[:], in_=d_b2))
    e_w1all = [P.dma("gp", lambda e, ei=ei: e.dma_start(
        out=w1all[:, ei * 128:(ei + 1) * 128],
        in_=d_w1[ei * 128:(ei + 1) * 128, :])) for ei in range(8)]
    e_w2all = [P.dma("gp", lambda e, ei=ei: e.dma_start(
        out=w2all[:, ei * 128:(ei + 1) * 128],
        in_=d_w2[ei * 128:(ei + 1) * 128, :])) for ei in range(8)]
    P.op("dve", lambda e: e.memset(cols[:], 0.0))
    P.op("dve", lambda e: e.memset(col(C_DEN), DENOM))
    P.op("dve", lambda e: e.memset(col(C_EPS), EPS))
    P.op("dve", lambda e: e.memset(col(C_1OD), 1.0 / D))
    P.op("dve", lambda e: e.memset(col(C_ONE), 1.0))
    e_la0 = P.op("dve", lambda e: e.memset(logacc[:], 0.0))
    e_or = P.op("dve", lambda e: e.memset(onesrow[:], 1.0))

    e_xT = []
    for dc in range(NCH):
        e_xT.append(P.dma("gp", lambda e, dc=dc: e.dma_start(
            out=bufA[:, dc * T:(dc + 1) * T],
            in_=d_xfeat[dc * 128:(dc + 1) * 128, :])))

    def wchunk_load(dst2, dsrc, ec, deps):
        slot = dst2[:, (ec % 2) * D:(ec % 2) * D + D]
        return P.dma("gp", lambda e: e.dma_start(
            out=slot, in_=dsrc[ec * 128:(ec + 1) * 128, :]), deps=deps)

    e_wk = [None] * NCH
    e_wv = [None] * NCH
    e_kmm = {}
    e_vmm = {}
    e_kev = {}
    e_pst = {}
    e_scan = [None] * NCH
    e_kvtc = [None] * NCH
    e_ccp = [None] * NCH

    for ec in range(2):
        e_wk[ec] = wchunk_load(wk2, d_wkT, ec, [])
        e_wv[ec] = wchunk_load(wv2, d_wvT, ec, [])

    for ec in range(NCH):
        wkslot = wk2[:, (ec % 2) * D:(ec % 2) * D + D]
        wvslot = wv2[:, (ec % 2) * D:(ec % 2) * D + D]
        for tb in range(NTB):
            for dc in range(NCH):
                e = P.op("pe", lambda e, tb=tb, dc=dc, wkslot=wkslot: e.matmul(
                    pk[:, tb * 512:(tb + 1) * 512],
                    wkslot[:, dc * 128:(dc + 1) * 128],
                    bufA[:, dc * T + tb * 512: dc * T + (tb + 1) * 512],
                    start=(dc == 0), stop=(dc == NCH - 1), skip_group_check=True),
                    deps=[e_wk[ec], e_xT[dc], e_kev.get((ec - 1, tb))])
            e_kmm[(ec, tb)] = e
            e_kev[(ec, tb)] = P.op("act", lambda e, tb=tb: e.activation(
                xw[:, tb * 512:(tb + 1) * 512], pk[:, tb * 512:(tb + 1) * 512], AF.Copy),
                deps=[e_kmm[(ec, tb)], e_pst.get((ec - 1, tb))])
            for dc in range(NCH):
                e = P.op("pe", lambda e, tb=tb, dc=dc, wvslot=wvslot: e.matmul(
                    pv[:, tb * 512:(tb + 1) * 512],
                    wvslot[:, dc * 128:(dc + 1) * 128],
                    bufA[:, dc * T + tb * 512: dc * T + (tb + 1) * 512],
                    start=(dc == 0), stop=(dc == NCH - 1), skip_group_check=True),
                    deps=[e_wv[ec], e_xT[dc], e_pst.get((ec - 1, tb))])
            e_vmm[(ec, tb)] = e
            if tb == NTB - 1 and ec + 2 < NCH:
                e_wk[ec + 2] = wchunk_load(wk2, d_wkT, ec + 2, [e_kmm[(ec, 3)]])
                e_wv[ec + 2] = wchunk_load(wv2, d_wvT, ec + 2, [e_vmm[(ec, 3)]])
            e_pst[(ec, tb)] = P.op("dve", lambda e, tb=tb: e.scalar_tensor_tensor(
                scr[:, tb * 512:(tb + 1) * 512], xw[:, tb * 512:(tb + 1) * 512],
                col(C_DEN), pv[:, tb * 512:(tb + 1) * 512], ALU.mult, ALU.mult),
                deps=[e_kev[(ec, tb)], e_vmm[(ec, tb)],
                      e_scan[ec - 1] if ec else None,
                      e_kvtc[ec - 1] if ec else None,
                      e_ccp[ec - 1] if ec else None])
        e_scan[ec] = P.op("dve", lambda e: e.tensor_tensor_scan(
            scr[:], scr[:], scr[:], 0.0, ALU.add, ALU.bypass),
            deps=[e_pst[(ec, tb)] for tb in range(NTB)])
        e_ccp[ec] = P.dma("sp", lambda e, ec=ec: e.dma_start(
            out=d_ci1[0:1, ec * 128:(ec + 1) * 128].rearrange("o (p u) -> (o p) u", u=1),
            in_=scr[:, T - 1:T]), deps=[e_scan[ec]])
        e_kvtc[ec] = P.op("dve", lambda e, ec=ec: e.tensor_copy(
            bufB[:, ec * T:(ec + 1) * T], scr[:]), deps=[e_scan[ec]])

    e_ci1 = e_ccp[NCH - 1]

    SG = 1
    e_ag1 = P.op("gp", lambda e: e.collective_compute(
        "AllGather", ALU.bypass, replica_groups=[list(range(NCORES))],
        ins=[d_ci1], outs=[d_co1[0:NCORES, :]]), deps=[e_ci1], kind="cc", seg=SG)
    e_ag1l = P.dma("sp", lambda e: e.dma_start(out=scr[0:8, :], in_=d_co1),
                   deps=[e_ag1, e_kvtc[NCH - 1]], seg=SG)

    e_wq = [None] * NCH
    e_qmm = {}
    e_qev = {}
    e_vh = {}
    e_ccmm = [None] * NCH
    e_ccev = None

    def load_wq(ec, deps):
        return P.dma("gp", lambda e, ec=ec: e.dma_start(
            out=wk2[:, (ec % 2) * D:(ec % 2) * D + D],
            in_=d_wqT[ec * 128:(ec + 1) * 128, :]), deps=deps, seg=SG)

    e_wq[0] = load_wq(0, [e_kmm[(NCH - 2, 3)]])
    e_wq[1] = load_wq(1, [e_kmm[(NCH - 1, 3)]])

    def emit_qmm(ec):
        wqslot = wk2[:, (ec % 2) * D:(ec % 2) * D + D]
        for tb in range(NTB):
            for dc in range(NCH):
                e = P.op("pe", lambda e, tb=tb, dc=dc, wqslot=wqslot: e.matmul(
                    pk[:, tb * 512:(tb + 1) * 512],
                    wqslot[:, dc * 128:(dc + 1) * 128],
                    bufA[:, dc * T + tb * 512: dc * T + (tb + 1) * 512],
                    start=(dc == 0), stop=(dc == NCH - 1), skip_group_check=True),
                    deps=[e_wq[ec], e_qev.get((ec - 1, tb)),
                          e_kev[(NCH - 1, tb)] if ec == 0 else None],
                    seg=SG)
            e_qmm[(ec, tb)] = e
            qoff = (ec % 2) * T
            e_qev[(ec, tb)] = P.op("act", lambda e, tb=tb, qoff=qoff: e.activation(
                qh[:, qoff + tb * 512:qoff + (tb + 1) * 512],
                pk[:, tb * 512:(tb + 1) * 512], AF.Copy),
                deps=[e_qmm[(ec, tb)], e_vh.get((ec - 2, tb))], seg=SG)

    def emit_vhat(ec):
        qoff = (ec % 2) * T
        for tb in range(NTB):
            e_vh[(ec, tb)] = P.op("dve", lambda e, ec=ec, tb=tb, qoff=qoff: e.scalar_tensor_tensor(
                bufB[:, ec * T + tb * 512: ec * T + (tb + 1) * 512],
                bufB[:, ec * T + tb * 512: ec * T + (tb + 1) * 512],
                ccols[:, ec:ec + 1],
                qh[:, qoff + tb * 512:qoff + (tb + 1) * 512], ALU.add, ALU.mult),
                deps=[e_qev[(ec, tb)], e_ccev], seg=SG)

    emit_qmm(0)
    e_wq[2] = load_wq(2, [e_qmm[(0, 3)]])
    emit_qmm(1)
    e_wq[3] = load_wq(3, [e_qmm[(1, 3)]])
    for c in range(NCH):
        e_ccmm[c] = P.op("pe", lambda e, c=c: e.matmul(
            pv[:, 32 + c:33 + c], scr[0:8, c * 128:(c + 1) * 128],
            sel1c[:], start=True, stop=True, skip_group_check=True),
            deps=[e_ag1l, e_sel1], seg=SG)
    e_ccev = P.op("dve", lambda e: e.tensor_copy(ccols[:], pv[:, 32:48]),
                  deps=[e_ccmm[NCH - 1]], seg=SG)
    emit_vhat(0)
    emit_vhat(1)
    for ec in range(2, NCH):
        emit_qmm(ec)
        if ec + 2 < NCH:
            e_wq[ec + 2] = load_wq(ec + 2, [e_qmm[(ec, 3)]])
        emit_vhat(ec)

    e_wo = [None] * NCH
    for ec in range(NCH):
        e_wo[ec] = P.dma("gp", lambda e, ec=ec: e.dma_start(
            out=bufA[:, ec * T:(ec + 1) * T],
            in_=d_woT[ec * 128:(ec + 1) * 128, :]),
            deps=[e_qmm[(NCH - 1, 3)]], seg=SG)

    e_xwl = {}
    e_womm = {}
    e_zev = {}
    e_x1 = {}
    e_x1b = {}
    e_x1st = {}
    e_x1bst = {}
    e_sw = {}
    e_rt = {}
    e_sq1 = {}
    e_racc = {}
    prev_ln_tail = None

    for ts in range(NTS):
        tb = ts // 4
        e_xwl[ts] = P.dma("sp", lambda e, ts=ts: e.dma_start(
            out=xw[:], in_=d_xtok[ts * 128:(ts + 1) * 128, :]),
            deps=[e_zev.get((ts - 1, 3)), e_pst[(NCH - 1, 3)] if ts == 0 else None],
            seg=SG)
        for ec in range(NCH):
            for fb in range(NTB):
                e = P.op("pe", lambda e, ts=ts, ec=ec, fb=fb: e.matmul(
                    pk[:, fb * 512:(fb + 1) * 512],
                    bufB[:, ec * T + ts * 128: ec * T + (ts + 1) * 128],
                    bufA[:, ec * T + fb * 512: ec * T + (fb + 1) * 512],
                    start=(ec == 0), stop=(ec == NCH - 1), skip_group_check=True),
                    deps=[e_wo[ec], e_vh[(ec, tb)],
                          e_zev.get((ts - 1, fb)),
                          e_qev[(NCH - 1, fb)] if ts == 0 and ec == 0 else None],
                    seg=SG)
            e_womm[(ts, ec)] = e
        for fb in range(NTB):
            e_zev[(ts, fb)] = P.op("dve", lambda e, fb=fb: e.tensor_add(
                scr[:, fb * 512:(fb + 1) * 512], pk[:, fb * 512:(fb + 1) * 512],
                xw[:, fb * 512:(fb + 1) * 512]),
                deps=[e_womm[(ts, NCH - 1)], e_xwl[ts],
                      e_ccmm[NCH - 1] if ts == 0 else None,
                      e_ag1l if ts == 0 else None],
                seg=SG)
        e_ms = P.op("dve", lambda e: e.reduce_sum(col(C_MS), scr[:], AX.X),
                    deps=[e_zev[(ts, fb)] for fb in range(NTB)], seg=SG)
        e_m = P.op("dve", lambda e: e.tensor_scalar_mul(col(C_M), col(C_MS), col(C_1OD)),
                   deps=[e_ms], seg=SG)
        e_sq1[ts] = P.op("act", lambda e: e.activation(
            pv[:], scr[:], AF.Square, accum_out=col(C_S2)),
            deps=[e_zev[(ts, fb)] for fb in range(NTB)] +
                 [e_ccev if ts == 0 else None],
            seg=SG)
        e_s2m = P.op("dve", lambda e: e.tensor_scalar_mul(col(C_S2M), col(C_S2), col(C_1OD)),
                     deps=[e_sq1[ts]], seg=SG)
        e_m2 = P.op("dve", lambda e: e.tensor_mul(col(C_M2), col(C_M), col(C_M)),
                    deps=[e_m], seg=SG)
        e_var = P.op("dve", lambda e: e.tensor_sub(col(C_VAR), col(C_S2M), col(C_M2)),
                     deps=[e_s2m, e_m2], seg=SG)
        e_std = P.op("act", lambda e: e.activation(col(C_STD), col(C_VAR), AF.Sqrt, bias=col(C_EPS)),
                     deps=[e_var], seg=SG)
        e_rstd = P.op("dve", lambda e: e.reciprocal(col(C_RSTD), col(C_STD)),
                      deps=[e_std], seg=SG)
        e_x1[ts] = P.op("dve", lambda e: e.tensor_scalar(
            x1t[:], scr[:], col(C_M), col(C_RSTD), ALU.subtract, ALU.mult),
            deps=[e_rstd, e_m, e_x1st.get(ts - 1)], seg=SG)
        slot = (ts % 2) * T
        e_x1b[ts] = P.op("dve", lambda e, slot=slot: e.tensor_copy(
            x1b2[:, slot:slot + T], x1t[:]),
            deps=[e_x1[ts], e_x1bst.get(ts - 2)], seg=SG)
        e_x1st[ts] = P.dma("sp", lambda e, ts=ts: e.dma_start(
            out=d_x1f[ts * 128:(ts + 1) * 128, :], in_=x1t[:]),
            deps=[e_x1[ts]], seg=SG)
        e_x1bst[ts] = P.dma("act", lambda e, ts=ts, slot=slot: e.dma_start(
            out=d_x1b[ts * 128:(ts + 1) * 128, :], in_=x1b2[:, slot:slot + T]),
            deps=[e_x1b[ts]], seg=SG)
        for ei in range(8):
            k = (ts * 8 + ei) % 2
            e_sw[(ts, ei)] = P.dma("gp", lambda e, ts=ts, ei=ei, k=k: e.dma_start(
                out=sww[:, k * D:(k + 1) * D],
                in_=d_sw[ei:ei + 1, (ts * 128) * D:(ts * 128 + 128) * D].rearrange(
                    "o (p d) -> (o p) d", p=128)),
                deps=[e_rt.get((ts, ei - 2)) if ei >= 2 else e_rt.get((ts - 1, 6 + ei))],
                seg=SG)
            rc = C_RA if (ts * 8 + ei) % 2 == 0 else C_RB
            e_rt[(ts, ei)] = P.op("dve", lambda e, k=k, slot=slot, rc=rc: e.scalar_tensor_tensor(
                sww[:, k * D:(k + 1) * D], sww[:, k * D:(k + 1) * D], col(C_ZERO),
                x1b2[:, slot:slot + T], ALU.add, ALU.mult, accum_out=col(rc)),
                deps=[e_sw[(ts, ei)], e_x1b[ts]], seg=SG)
            e_racc[(ts, ei)] = P.op("dve", lambda e, ei=ei, rc=rc: e.tensor_add(
                logacc[:, ei:ei + 1], logacc[:, ei:ei + 1], col(rc)),
                deps=[e_rt[(ts, ei)], e_la0 if ts == 0 else None], seg=SG)

    e_lmm = P.op("pe", lambda e: e.matmul(
        pv[0:1, 0:8], col(C_ONE), logacc[:], start=True, stop=True, skip_group_check=True),
        deps=[e_racc[(NTS - 1, 7)], e_sq1[NTS - 1]], seg=SG)
    e_lgr = P.op("dve", lambda e: e.tensor_copy(row(R_LG), pv[0:1, 0:8]), deps=[e_lmm], seg=SG)
    e_ci2 = P.dma("sp", lambda e: e.dma_start(out=d_ci2, in_=row(R_LG)), deps=[e_lgr], seg=SG)

    SG = 2
    e_ag2 = P.op("gp", lambda e: e.collective_compute(
        "AllGather", ALU.bypass, replica_groups=[list(range(NCORES))],
        ins=[d_ci2], outs=[d_co2[0:NCORES, :]]), deps=[e_ci2], kind="cc", seg=SG)
    e_ag2l = P.dma("sp", lambda e: e.dma_start(out=ag2s[:], in_=d_co2),
                   deps=[e_ag2], seg=SG)
    e_smm = P.op("pe", lambda e: e.matmul(
        pv[0:1, 8:16], sel2c[:], ag2s[:], start=True, stop=True, skip_group_check=True),
        deps=[e_ag2l, e_sel2, e_lmm], seg=SG)
    e_l2 = P.op("dve", lambda e: e.tensor_add(row(R_L2), pv[0:1, 8:16], swbrow[:]),
                deps=[e_smm, e_swb, e_lgr], seg=SG)
    e_nm = P.op("dve", lambda e: e.reduce_max(sc(S_NEGMAX), row(R_L2), AX.X, negate=True),
                deps=[e_l2], seg=SG)
    e_ex = P.op("act", lambda e: e.activation(row(R_EXP), row(R_L2), AF.Exp, bias=sc(S_NEGMAX)),
                deps=[e_nm, e_l2], seg=SG)
    e_sm = P.op("dve", lambda e: e.reduce_sum(sc(S_SUM), row(R_EXP), AX.X), deps=[e_ex], seg=SG)
    e_rs = P.op("dve", lambda e: e.reciprocal(sc(S_RSUM), sc(S_SUM)), deps=[e_sm], seg=SG)
    e_pr = P.op("dve", lambda e: e.tensor_scalar_mul(row(R_PR), row(R_EXP), sc(S_RSUM)),
                deps=[e_ex, e_rs], seg=SG)
    e_m1 = P.op("dve", lambda e: e.reduce_max(sc(S_M1), row(R_PR), AX.X), deps=[e_pr], seg=SG)
    e_oh1 = P.op("dve", lambda e: e.tensor_single_scalar(row(R_OH1), row(R_PR), sc(S_M1), ALU.is_ge),
                 deps=[e_m1, e_pr], seg=SG)
    e_t1 = P.op("dve", lambda e: e.tensor_mul(row(R_T1), row(R_OH1), row(R_PR)), deps=[e_oh1], seg=SG)
    e_pm = P.op("dve", lambda e: e.tensor_sub(row(R_PM), row(R_PR), row(R_T1)), deps=[e_t1], seg=SG)
    e_m2_ = P.op("dve", lambda e: e.reduce_max(sc(S_M2), row(R_PM), AX.X), deps=[e_pm], seg=SG)
    e_oh2 = P.op("dve", lambda e: e.tensor_single_scalar(row(R_OH2), row(R_PM), sc(S_M2), ALU.is_ge),
                 deps=[e_m2_, e_pm], seg=SG)
    e_st1 = P.op("pe", lambda e: e.transpose(pv[0:8, 16:17], row(R_OH1), identc[0:1, 0:1]),
                 deps=[e_oh1, e_smm, e_ident], seg=SG)
    e_st2 = P.op("pe", lambda e: e.transpose(pv[0:8, 17:18], row(R_OH2), identc[0:1, 0:1]),
                 deps=[e_oh2], seg=SG)
    e_selT = P.op("dve", lambda e: e.tensor_copy(selT[:], pv[0:8, 16:18]),
                  deps=[e_st1, e_st2], seg=SG)
    e_sb1 = P.op("pe", lambda e: e.matmul(pv[:, 32:40], onesrow[:], row(R_OH1),
                                          start=True, stop=True, skip_group_check=True),
                 deps=[e_oh1, e_or, e_st1, e_selT], seg=SG)
    e_sb2 = P.op("pe", lambda e: e.matmul(pv[:, 40:48], onesrow[:], row(R_OH2),
                                          start=True, stop=True, skip_group_check=True),
                 deps=[e_oh2, e_sb1], seg=SG)
    e_selbc = P.op("dve", lambda e: e.tensor_copy(selbc[:, 0:16], pv[:, 32:48]),
                   deps=[e_sb1, e_sb2], seg=SG)
    e_g1 = P.op("pe", lambda e: e.matmul(pv[:, 48:49], onesrow[:], sc(S_M1),
                                         start=True, stop=True, skip_group_check=True),
                deps=[e_m1, e_sb2, e_selbc], seg=SG)
    e_g2 = P.op("pe", lambda e: e.matmul(pv[:, 49:50], onesrow[:], sc(S_M2),
                                         start=True, stop=True, skip_group_check=True),
                deps=[e_m2_, e_g1], seg=SG)
    e_gbc = P.op("dve", lambda e: e.tensor_copy(gbc[:], pv[:, 48:50]),
                 deps=[e_g1, e_g2], seg=SG)
    e_gath = []
    for k in range(2):
        ek0 = P.op("dve", lambda e, k=k: e.memset(w1fd[:, k * 128:(k + 1) * 128], 0.0), seg=SG)
        ek1 = P.op("dve", lambda e, k=k: e.memset(w1fd[:, (2 + k) * 128:(3 + k) * 128], 0.0), seg=SG)
        last0, last1 = ek0, ek1
        for ei in range(8):
            last0 = P.op("dve", lambda e, k=k, ei=ei: e.scalar_tensor_tensor(
                w1fd[:, k * 128:(k + 1) * 128], w1all[:, ei * 128:(ei + 1) * 128],
                selbc[:, k * 8 + ei:k * 8 + ei + 1], w1fd[:, k * 128:(k + 1) * 128],
                ALU.mult, ALU.add),
                deps=[e_w1all[ei], e_selbc, last0], seg=SG)
            last1 = P.op("dve", lambda e, k=k, ei=ei: e.scalar_tensor_tensor(
                w1fd[:, (2 + k) * 128:(3 + k) * 128], w2all[:, ei * 128:(ei + 1) * 128],
                selbc[:, k * 8 + ei:k * 8 + ei + 1], w1fd[:, (2 + k) * 128:(3 + k) * 128],
                ALU.mult, ALU.add),
                deps=[e_w2all[ei], e_selbc, last1], seg=SG)
        e_gath.append((last0, last1))
    e_wfd_parts = [P.dma("act", lambda e, c=c: e.dma_start(
        out=d_wfd[c * 128:(c + 1) * 128, :],
        in_=w1fd[:, c * 128:(c + 1) * 128]),
        deps=[e_gath[0][0], e_gath[0][1], e_gath[1][0], e_gath[1][1]], seg=SG)
        for c in range(4)]
    e_wfd_st = e_wfd_parts[-1]
    e_wT = []
    for c in range(4):
        e_wT.append(P.dma("act", lambda e, c=c: e.dma_start(
            out=wexp[:, c * 128:(c + 1) * 128],
            in_=d_wfd[c * 128:(c + 1) * 128, :], transpose=True),
            deps=[e_wfd_st], seg=SG))
    e_w2s = []
    for k in range(2):
        e_w2s.append(P.op("dve", lambda e, k=k: e.tensor_scalar_mul(
            wexp[:, (2 + k) * 128:(3 + k) * 128], wexp[:, (2 + k) * 128:(3 + k) * 128],
            gbc[:, k:k + 1]), deps=[e_wT[2 + k], e_gbc], seg=SG))
    e_bg1 = P.op("pe", lambda e: e.matmul(pv[:, 52:54], b1sb[:], selT[:],
                                          start=True, stop=True, skip_group_check=True),
                 deps=[e_selT, e_b1, e_g2, e_gbc], seg=SG)
    e_bg2 = P.op("pe", lambda e: e.matmul(pv[:, 54:56], b2sb[:], selT[:],
                                          start=True, stop=True, skip_group_check=True),
                 deps=[e_selT, e_b2, e_bg1], seg=SG)
    e_b1c = P.op("dve", lambda e: e.tensor_copy(b1cols[:], pv[:, 52:54]), deps=[e_bg1], seg=SG)
    e_b2c = P.op("dve", lambda e: e.tensor_copy(b2cols[:], pv[:, 54:56]), deps=[e_bg2], seg=SG)
    e_bt = P.op("dve", lambda e: e.tensor_scalar_mul(col(C_RA), b2cols[:, 0:1], gbc[:, 0:1]),
                deps=[e_b2c, e_gbc], seg=SG)
    e_bc2 = P.op("dve", lambda e: e.scalar_tensor_tensor(
        bc2col[:], b2cols[:, 1:2], gbc[:, 1:2], col(C_RA), ALU.mult, ALU.add),
        deps=[e_bt], seg=SG)

    e_w3 = [None] * NCH
    for dc in range(NCH):
        e_w3[dc] = P.dma("gp", lambda e, dc=dc: e.dma_start(
            out=bufA[:, dc * T:(dc + 1) * T],
            in_=d_w3T[dc * 128:(dc + 1) * 128, :]),
            deps=[e_womm[(NTS - 1, dc)]], seg=SG)
    e_x1T = [None] * NCH
    for dc in range(NCH):
        e_x1T[dc] = P.dma("act", lambda e, dc=dc: e.dma_start(
            out=bufB[:, dc * T:(dc + 1) * T],
            in_=d_x1b[:, dc * 128:(dc + 1) * 128], transpose=True),
            deps=[e_x1bst[NTS - 1], e_womm[(NTS - 1, dc)], e_wfd_st], seg=SG)

    uslot = x1b2[:, 0:T]
    x1T3d = bufB[:].rearrange("p (dc t) -> p dc t", dc=NCH)
    e_xw3 = {}
    e_hmm = {}
    e_gel = {}
    e_omm = {}
    e_u1 = {}
    e_u = {}
    e_ymm = {}
    e_z2 = {}
    e_sq2 = {}
    e_outw = {}
    e_ost = {}

    for ts in range(NTS):
        e_xw3[ts] = P.dma("sp", lambda e, ts=ts: e.dma_start(
            out=xw[:], in_=d_x1f[ts * 128:(ts + 1) * 128, :]),
            deps=[e_x1st[NTS - 1] if ts == 0 else None,
                  e_zev[(NTS - 1, 3)] if ts == 0 else None,
                  e_z2.get((ts - 1, 3))], seg=SG)
        for k in range(2):
            for h in range(NCH):
                e = P.op("pe", lambda e, k=k, h=h, ts=ts: e.matmul(
                    pv[:, h * 128:(h + 1) * 128],
                    wexp[:, k * 128:(k + 1) * 128],
                    bufB[:, h * T + ts * 128: h * T + (ts + 1) * 128],
                    start=True, stop=True, skip_group_check=True),
                    deps=[e_wT[k], e_x1T[h],
                          (e_bg2 if ts == 0 and k == 0 else None) if h == 0 else None,
                          (e_b2c if ts == 0 and k == 0 else None) if h == 0 else None,
                          (e_gel[(ts, 0)] if k == 1 else None) if h == 0 else None,
                          e_z2.get((ts - 1, h // 4)) if k == 0 else None],
                    seg=SG)
            e_hmm[(ts, k)] = e
            e_gel[(ts, k)] = P.op("act", lambda e, k=k: e.activation(
                qh[:, 0:T], pv[:], AF.Gelu, bias=b1cols[:, k:k + 1]),
                deps=[e_hmm[(ts, k)], e_b1c,
                      e_omm.get((ts, k - 1)),
                      e_u.get(ts - 1) if k == 0 else None],
                seg=SG)
            for h in range(NCH):
                e = P.op("pe", lambda e, k=k, h=h: e.matmul(
                    pk[:, h * 128:(h + 1) * 128],
                    wexp[:, (2 + k) * 128:(3 + k) * 128],
                    qh[:, h * 128:(h + 1) * 128],
                    start=(k == 0 and h % 4 == 0),
                    stop=(k == 1 and h % 4 == 3), skip_group_check=True),
                    deps=[e_gel[(ts, k)], e_w2s[k],
                          e_sq2.get(ts - 1) if k == 0 and h == 0 else None],
                    seg=SG)
            e_omm[(ts, k)] = e
        e_u1[ts] = P.op("dve", lambda e: e.tensor_scalar_add(qh[:, 0:T], pk[:], bc2col[:]),
                        deps=[e_omm[(ts, 1)], e_bc2, e_gel[(ts, 1)]], seg=SG)
        e_u[ts] = P.op("dve", lambda e, ts=ts: e.tensor_add(
            uslot[:], qh[:, 0:T], x1T3d[:, :, ts * 128:(ts + 1) * 128]),
            deps=[e_u1[ts], e_ymm.get((ts - 1, 3)),
                  e_x1bst[NTS - 2] if ts == 0 else None] +
                 ([d for d in e_x1T] if ts == 0 else []), seg=SG)
        for fb in range(NTB):
            for dc in range(NCH):
                e = P.op("pe", lambda e, fb=fb, dc=dc: e.matmul(
                    pv[:, fb * 512:(fb + 1) * 512],
                    uslot[:, dc * 128:(dc + 1) * 128],
                    bufA[:, dc * T + fb * 512: dc * T + (fb + 1) * 512],
                    start=(dc == 0), stop=False, skip_group_check=True),
                    deps=[e_u[ts], e_w3[dc], e_gel[(ts, 1)],
                          e_z2.get((ts - 1, fb))], seg=SG)
            e_ymm[(ts, fb)] = P.op("pe", lambda e, fb=fb: e.matmul(
                pv[:, fb * 512:(fb + 1) * 512], onesb3[:],
                b3row[0:1, fb * 512:(fb + 1) * 512],
                start=False, stop=True, skip_group_check=True),
                deps=[e, e_b3, e_ob3], seg=SG)
        for fb in range(NTB):
            e_z2[(ts, fb)] = P.op("dve", lambda e, fb=fb: e.tensor_add(
                scr[:, fb * 512:(fb + 1) * 512], pv[:, fb * 512:(fb + 1) * 512],
                xw[:, fb * 512:(fb + 1) * 512]),
                deps=[e_ymm[(ts, fb)], e_xw3[ts],
                      e_lgr if ts == 0 and fb == 0 else None], seg=SG)
        e_ms2 = P.op("dve", lambda e: e.reduce_sum(col(C_MS), scr[:], AX.X),
                     deps=[e_z2[(ts, fb)] for fb in range(NTB)], seg=SG)
        e_mm2 = P.op("dve", lambda e: e.tensor_scalar_mul(col(C_M), col(C_MS), col(C_1OD)),
                     deps=[e_ms2], seg=SG)
        e_sq2[ts] = P.op("act", lambda e: e.activation(
            pk[:], scr[:], AF.Square, accum_out=col(C_S2)),
            deps=[e_z2[(ts, fb)] for fb in range(NTB)] + [e_u1[ts]], seg=SG)
        e_s2m2 = P.op("dve", lambda e: e.tensor_scalar_mul(col(C_S2M), col(C_S2), col(C_1OD)),
                      deps=[e_sq2[ts]], seg=SG)
        e_mq2 = P.op("dve", lambda e: e.tensor_mul(col(C_M2), col(C_M), col(C_M)),
                     deps=[e_mm2], seg=SG)
        e_var2 = P.op("dve", lambda e: e.tensor_sub(col(C_VAR), col(C_S2M), col(C_M2)),
                      deps=[e_s2m2, e_mq2], seg=SG)
        e_std2 = P.op("act", lambda e: e.activation(col(C_STD), col(C_VAR), AF.Sqrt, bias=col(C_EPS)),
                      deps=[e_var2], seg=SG)
        e_rstd2 = P.op("dve", lambda e: e.reciprocal(col(C_RSTD), col(C_STD)),
                       deps=[e_std2], seg=SG)
        e_outw[ts] = P.op("dve", lambda e: e.tensor_scalar(
            x1t[:], scr[:], col(C_M), col(C_RSTD), ALU.subtract, ALU.mult),
            deps=[e_rstd2, e_mm2, e_ost.get(ts - 1), e_x1st[NTS - 1] if ts == 0 else None],
            seg=SG)
        e_ost[ts] = P.dma("sp", lambda e, ts=ts: e.dma_start(
            out=d_out[ts * 128:(ts + 1) * 128, :], in_=x1t[:]),
            deps=[e_outw[ts]], seg=SG)

    P.op("sp", lambda e: e.nop(), deps=[e_ost[NTS - 1]], seg=SG)

    P.emit(nc, block, sems)

    for h in reversed(ctx):
        h.__exit__(None, None, None)
    return nc


_NC_CACHE = None


def kernel(L0_attn_x, L0_attn_Wq, L0_attn_Wk, L0_attn_Wv, L0_attn_Wo,
           L0_ffn_switch_w, L0_ffn_switch_b, L0_ffn_w1, L0_ffn_b1,
           L0_ffn_w2, L0_ffn_b2, L0_ffn_w3, L0_ffn_b3):
    global _NC_CACHE
    f32 = np.float32
    x = np.ascontiguousarray(np.asarray(L0_attn_x, dtype=f32)).reshape(NCORES, T, D)
    xfeat = np.ascontiguousarray(np.swapaxes(x, 1, 2))
    def slab(w):
        w4 = np.asarray(w, dtype=f32).reshape(16, 128, 16, 128)
        return np.ascontiguousarray(w4.transpose(0, 3, 2, 1).reshape(2048, 2048))
    wqT = slab(L0_attn_Wq)
    wkT = slab(L0_attn_Wk)
    wvT = slab(L0_attn_Wv)
    woT = np.ascontiguousarray(np.asarray(L0_attn_Wo, dtype=f32).T)
    w3T = np.ascontiguousarray(np.asarray(L0_ffn_w3, dtype=f32).T)
    sw = np.ascontiguousarray(np.asarray(L0_ffn_switch_w, dtype=f32))
    swb = np.asarray(L0_ffn_switch_b, dtype=f32).reshape(1, 8)
    w1 = np.ascontiguousarray(np.asarray(L0_ffn_w1, dtype=f32).reshape(8 * 128, 128))
    w2 = np.ascontiguousarray(np.asarray(L0_ffn_w2, dtype=f32).reshape(8 * 128, 128))
    b1 = np.ascontiguousarray(np.asarray(L0_ffn_b1, dtype=f32))
    b2 = np.ascontiguousarray(np.asarray(L0_ffn_b2, dtype=f32))
    b3 = np.asarray(L0_ffn_b3, dtype=f32).reshape(1, D)

    if _NC_CACHE is None:
        _NC_CACHE = build_nc()
    nc = _NC_CACHE

    in_maps = []
    for c in range(NCORES):
        sel1 = np.zeros((8, 1), dtype=f32)
        if c % 2 == 1:
            sel1[c - 1, 0] = 1.0
        sel2 = np.zeros((8, 1), dtype=f32)
        sel2[(c // 2) * 2, 0] = 1.0
        in_maps.append({
            "xtok": x[c], "xfeat": xfeat[c],
            "wqT": wqT, "wkT": wkT, "wvT": wvT, "woT": woT, "w3T": w3T,
            "sw": sw, "swb": swb, "w1": w1, "w2": w2, "b1": b1, "b2": b2,
            "b3": b3, "sel1": sel1, "sel2": sel2,
        })
    res = run_bass_kernel_spmd(nc, in_maps, core_ids=list(range(NCORES)))
    out = np.stack([np.asarray(res.results[c]["out"]) for c in range(NCORES)], axis=0)
    return out.reshape(4, 4096, D).astype(np.float32)


# version 06
# speedup vs baseline: 1.0060x; 1.0060x over previous
import math
import numpy as np
import ml_dtypes
import concourse.bass as bass
import concourse.mybir as mybir
from concourse.bass_utils import run_bass_kernel_spmd

F32 = mybir.dt.float32
BF16 = mybir.dt.bfloat16
AF = mybir.ActivationFunctionType
ALU = mybir.AluOpType
AX = mybir.AxisListType

NCORES = 8
D = 2048
T = 2048
NCH = 16
NTS = 16
NTB = 4
DENOM = math.sqrt(2048.0) / 4096.0
EPS = 1e-5

ENGMAP = {"pe": "tensor", "dve": "vector", "act": "scalar", "gp": "gpsimd", "sp": "sync"}


class Prog:

    def __init__(self):
        self.ops = []

    def op(self, eng, fn, deps=(), kind="c", seg=0):
        i = len(self.ops)
        self.ops.append({
            "eng": eng, "fn": fn, "kind": kind, "seg": seg,
            "deps": [d for d in deps if d is not None],
        })
        return i

    def dma(self, eng, fn, deps=(), seg=0):
        return self.op(eng, fn, deps, kind="d", seg=seg)

    def emit(self, nc, block, sems):
        ops = self.ops
        def semkey(o):
            if o["kind"] == "cc":
                return "cc"
            if o["kind"] == "d":
                return o["eng"] + "q"
            return o["eng"]
        targets = set()
        for o in ops:
            for d in o["deps"]:
                targets.add(d)
        counts = {}
        for i, o in enumerate(ops):
            o["inc"] = 0
            if i in targets or o["kind"] in ("d", "cc"):
                k = semkey(o)
                step = 16 if o["kind"] == "d" else 1
                counts[k] = counts.get(k, 0) + step
                o["inc"] = step
                o["semk"] = k
                o["val"] = counts[k]
        nsegs = 1 + max(o["seg"] for o in ops)
        for seg in range(nsegs):
            for eng in ("gp", "sp", "act", "dve", "pe"):
                mine = [o for o in ops if o["seg"] == seg and o["eng"] == eng]
                if not mine:
                    continue
                def body(e, mine=mine, eng=eng):
                    marks = {}
                    for o in mine:
                        for d in o["deps"]:
                            p = ops[d]
                            if p["inc"] == 0:
                                raise RuntimeError("dep on non-target op")
                            k, v = p["semk"], p["val"]
                            if marks.get(k, -1) >= v:
                                continue
                            e.wait_ge(sems[k], v)
                            marks[k] = v
                        inst = o["fn"](e)
                        if o["inc"]:
                            inst.then_inc(sems[o["semk"]], o["inc"])
                getattr(block, ENGMAP[eng])(body)


def build_nc():
    nc = bass.Bass("TRN2", target_bir_lowering=False)

    d_xtok = nc.dram_tensor("xtok", [T, D], F32, kind="ExternalInput").ap()
    d_xfeat = nc.dram_tensor("xfeat", [D, T], F32, kind="ExternalInput").ap()
    d_wqT = nc.dram_tensor("wqT", [D, D], F32, kind="ExternalInput").ap()
    d_wkT = nc.dram_tensor("wkT", [D, D], F32, kind="ExternalInput").ap()
    d_wvT = nc.dram_tensor("wvT", [D, D], F32, kind="ExternalInput").ap()
    d_woT = nc.dram_tensor("woT", [D, D], F32, kind="ExternalInput").ap()
    d_w3T = nc.dram_tensor("w3T", [D, D], F32, kind="ExternalInput").ap()
    d_sw = nc.dram_tensor("sw", [8, T * D], F32, kind="ExternalInput").ap()
    d_swb = nc.dram_tensor("swb", [1, 8], F32, kind="ExternalInput").ap()
    d_w1 = nc.dram_tensor("w1", [8 * 128, 128], F32, kind="ExternalInput").ap()
    d_w2 = nc.dram_tensor("w2", [8 * 128, 128], F32, kind="ExternalInput").ap()
    d_b1 = nc.dram_tensor("b1", [8, 128], F32, kind="ExternalInput").ap()
    d_b2 = nc.dram_tensor("b2", [8, 128], F32, kind="ExternalInput").ap()
    d_b3 = nc.dram_tensor("b3", [1, D], F32, kind="ExternalInput").ap()
    d_sel1 = nc.dram_tensor("sel1", [8, 1], F32, kind="ExternalInput").ap()
    d_sel2 = nc.dram_tensor("sel2", [8, 1], F32, kind="ExternalInput").ap()
    d_out = nc.dram_tensor("out", [T, D], F32, kind="ExternalOutput").ap()

    d_x1f = nc.dram_tensor("x1f", [T, D], F32, kind="Internal").ap()
    d_x1b = nc.dram_tensor("x1b", [T, D], BF16, kind="Internal").ap()
    d_wfd = nc.dram_tensor("wfd", [4 * 128, 128], BF16, kind="Internal").ap()
    d_ci1 = nc.dram_tensor("ci1", [1, D], F32, kind="Internal").ap()
    d_co1 = nc.dram_tensor("co1", [8, D], F32, kind="Internal").ap()
    d_ci2 = nc.dram_tensor("ci2", [1, 8], F32, kind="Internal").ap()
    d_co2 = nc.dram_tensor("co2", [8, 8], F32, kind="Internal").ap()

    ctx = []
    def sb(name, shape, dt):
        h = nc.sbuf_tensor(name, shape, dt)
        t = h.__enter__(); ctx.append(h); return t
    def psum(name, shape, dt=F32):
        h = nc.psum_tensor(name, shape, dt)
        t = h.__enter__(); ctx.append(h); return t

    bufA = sb("bufA", [128, NCH * T], BF16)
    bufB = sb("bufB", [128, NCH * T], BF16)
    wk2 = sb("wk2", [128, 2 * D], BF16)
    wv2 = sb("wv2", [128, 2 * D], BF16)
    sww = sb("sww", [128, 2 * D], BF16)
    qh = sb("qh", [128, 2 * T], BF16)
    x1b2 = sb("x1b2", [128, 2 * T], BF16)
    wexp = sb("wexp", [128, 4 * 128], BF16)
    w1all = sb("w1all", [128, 8 * 128], BF16)
    w2all = sb("w2all", [128, 8 * 128], BF16)
    w1fd = sb("w1fd", [128, 4 * 128], BF16)

    xw = sb("xw", [128, T], F32)
    scr = sb("scr", [128, T], F32)
    x1t = sb("x1t", [128, T], F32)
    b3row = sb("b3row", [1, D], BF16)
    onesb3 = sb("onesb3", [1, 128], BF16)
    identc = sb("identc", [128, 1], F32)
    ccols = sb("ccols", [128, 16], F32)
    ag2s = sb("ag2s", [8, 8], F32)
    selT = sb("selT", [8, 2], F32)
    selbc = sb("selbc", [128, 16], F32)
    b1sb = sb("b1sb", [8, 128], F32)
    b2sb = sb("b2sb", [8, 128], F32)
    b1cols = sb("b1cols", [128, 2], F32)
    b2cols = sb("b2cols", [128, 2], F32)
    bc2col = sb("bc2col", [128, 1], F32)
    gbc = sb("gbc", [128, 2], F32)
    logacc = sb("logacc", [128, 8], F32)
    cols = sb("cols", [128, 16], F32)
    C_DEN, C_ZERO, C_EPS, C_1OD, C_M, C_RSTD, C_S2, C_VAR, C_MS, C_STD, C_M2, C_S2M, C_RA, C_RB, C_ONE = range(15)
    rows = sb("rows", [1, 64], F32)
    R_LG, R_L2, R_EXP, R_PR, R_OH1, R_OH2, R_PM, R_T1 = range(8)
    s11 = sb("s11", [1, 8], F32)
    S_NEGMAX, S_SUM, S_RSUM, S_M1, S_M2 = range(5)
    onesrow = sb("onesrow", [1, 128], F32)
    swbrow = sb("swbrow", [1, 8], F32)
    sel1c = sb("sel1c", [8, 1], F32)
    sel2c = sb("sel2c", [8, 1], F32)

    pk = psum("pk", [128, T], F32)
    pv = psum("pv", [128, T], F32)

    sems = {}
    for name in ("pe", "dve", "act", "gpq", "spq", "actq", "cc"):
        h = nc.semaphore(name)
        sems[name] = h.__enter__(); ctx.append(h)

    block_h = nc.Block(); block = block_h.__enter__(); ctx.append(block_h)

    P = Prog()
    def col(i):
        return cols[:, i:i + 1]
    def row(i):
        return rows[:, 8 * i:8 * i + 8]
    def sc(i):
        return s11[:, i:i + 1]

    e_ident = P.op("dve", lambda e: e.memset(identc[:], 1.0))
    e_b3 = P.dma("gp", lambda e: e.dma_start(out=b3row[:], in_=d_b3))
    e_ob3 = P.op("dve", lambda e: e.memset(onesb3[:], 1.0))
    e_swb = P.dma("sp", lambda e: e.dma_start(out=swbrow[:], in_=d_swb))
    e_sel1 = P.dma("sp", lambda e: e.dma_start(out=sel1c[:], in_=d_sel1))
    e_sel2 = P.dma("sp", lambda e: e.dma_start(out=sel2c[:], in_=d_sel2))
    e_b1 = P.dma("sp", lambda e: e.dma_start(out=b1sb[:], in_=d_b1))
    e_b2 = P.dma("sp", lambda e: e.dma_start(out=b2sb[:], in_=d_b2))
    e_w1all = [P.dma("gp", lambda e, ei=ei: e.dma_start(
        out=w1all[:, ei * 128:(ei + 1) * 128],
        in_=d_w1[ei * 128:(ei + 1) * 128, :])) for ei in range(8)]
    e_w2all = [P.dma("gp", lambda e, ei=ei: e.dma_start(
        out=w2all[:, ei * 128:(ei + 1) * 128],
        in_=d_w2[ei * 128:(ei + 1) * 128, :])) for ei in range(8)]
    e_cz = P.op("dve", lambda e: e.memset(cols[:], 0.0))
    e_cz = P.op("dve", lambda e: e.memset(col(C_DEN), DENOM), deps=[e_cz])
    e_cz = P.op("dve", lambda e: e.memset(col(C_EPS), EPS), deps=[e_cz])
    e_cz = P.op("dve", lambda e: e.memset(col(C_1OD), 1.0 / D), deps=[e_cz])
    e_cz = P.op("dve", lambda e: e.memset(col(C_ONE), 1.0), deps=[e_cz])
    e_la0 = P.op("dve", lambda e: e.memset(logacc[:], 0.0))
    e_or = P.op("dve", lambda e: e.memset(onesrow[:], 1.0))

    e_xT = []
    for dc in range(NCH):
        e_xT.append(P.dma("gp", lambda e, dc=dc: e.dma_start(
            out=bufA[:, dc * T:(dc + 1) * T],
            in_=d_xfeat[dc * 128:(dc + 1) * 128, :])))

    def wchunk_load(dst2, dsrc, ec, deps):
        slot = dst2[:, (ec % 2) * D:(ec % 2) * D + D]
        return P.dma("gp", lambda e: e.dma_start(
            out=slot, in_=dsrc[ec * 128:(ec + 1) * 128, :]), deps=deps)

    e_wk = [None] * NCH
    e_wv = [None] * NCH
    e_kmm = {}
    e_vmm = {}
    e_kev = {}
    e_pst = {}
    e_scan = [None] * NCH
    e_kvtc = [None] * NCH
    e_ccp = [None] * NCH

    for ec in range(2):
        e_wk[ec] = wchunk_load(wk2, d_wkT, ec, [])
        e_wv[ec] = wchunk_load(wv2, d_wvT, ec, [])

    for ec in range(NCH):
        wkslot = wk2[:, (ec % 2) * D:(ec % 2) * D + D]
        wvslot = wv2[:, (ec % 2) * D:(ec % 2) * D + D]
        for tb in range(NTB):
            for dc in range(NCH):
                e = P.op("pe", lambda e, tb=tb, dc=dc, wkslot=wkslot: e.matmul(
                    pk[:, tb * 512:(tb + 1) * 512],
                    wkslot[:, dc * 128:(dc + 1) * 128],
                    bufA[:, dc * T + tb * 512: dc * T + (tb + 1) * 512],
                    start=(dc == 0), stop=(dc == NCH - 1), skip_group_check=True),
                    deps=[e_wk[ec], e_xT[dc], e_kev.get((ec - 1, tb))])
            e_kmm[(ec, tb)] = e
            e_kev[(ec, tb)] = P.op("act", lambda e, tb=tb: e.activation(
                xw[:, tb * 512:(tb + 1) * 512], pk[:, tb * 512:(tb + 1) * 512], AF.Copy),
                deps=[e_kmm[(ec, tb)], e_pst.get((ec - 1, tb))])
            for dc in range(NCH):
                e = P.op("pe", lambda e, tb=tb, dc=dc, wvslot=wvslot: e.matmul(
                    pv[:, tb * 512:(tb + 1) * 512],
                    wvslot[:, dc * 128:(dc + 1) * 128],
                    bufA[:, dc * T + tb * 512: dc * T + (tb + 1) * 512],
                    start=(dc == 0), stop=(dc == NCH - 1), skip_group_check=True),
                    deps=[e_wv[ec], e_xT[dc], e_pst.get((ec - 1, tb))])
            e_vmm[(ec, tb)] = e
            if tb == NTB - 1 and ec + 2 < NCH:
                e_wk[ec + 2] = wchunk_load(wk2, d_wkT, ec + 2, [e_kmm[(ec, 3)]])
                e_wv[ec + 2] = wchunk_load(wv2, d_wvT, ec + 2, [e_vmm[(ec, 3)]])
            e_pst[(ec, tb)] = P.op("dve", lambda e, tb=tb: e.scalar_tensor_tensor(
                scr[:, tb * 512:(tb + 1) * 512], xw[:, tb * 512:(tb + 1) * 512],
                col(C_DEN), pv[:, tb * 512:(tb + 1) * 512], ALU.mult, ALU.mult),
                deps=[e_kev[(ec, tb)], e_vmm[(ec, tb)],
                      e_scan[ec - 1] if ec else None,
                      e_kvtc[ec - 1] if ec else None,
                      e_ccp[ec - 1] if ec else None])
        e_scan[ec] = P.op("dve", lambda e: e.tensor_tensor_scan(
            scr[:], scr[:], scr[:], 0.0, ALU.add, ALU.bypass),
            deps=[e_pst[(ec, tb)] for tb in range(NTB)])
        e_ccp[ec] = P.dma("sp", lambda e, ec=ec: e.dma_start(
            out=d_ci1[0:1, ec * 128:(ec + 1) * 128].rearrange("o (p u) -> (o p) u", u=1),
            in_=scr[:, T - 1:T]), deps=[e_scan[ec]])
        e_kvtc[ec] = P.op("dve", lambda e, ec=ec: e.tensor_copy(
            bufB[:, ec * T:(ec + 1) * T], scr[:]), deps=[e_scan[ec]])

    e_ci1 = e_ccp[NCH - 1]

    SG = 1
    e_ag1 = P.op("gp", lambda e: e.collective_compute(
        "AllGather", ALU.bypass, replica_groups=[list(range(NCORES))],
        ins=[d_ci1], outs=[d_co1[0:NCORES, :]]), deps=[e_ci1], kind="cc", seg=SG)
    e_ag1l = P.dma("sp", lambda e: e.dma_start(out=scr[0:8, :], in_=d_co1),
                   deps=[e_ag1, e_kvtc[NCH - 1]], seg=SG)

    e_wq = [None] * NCH
    e_qmm = {}
    e_qev = {}
    e_vh = {}
    e_ccmm = [None] * NCH
    e_ccev = None

    def load_wq(ec, deps):
        return P.dma("gp", lambda e, ec=ec: e.dma_start(
            out=wk2[:, (ec % 2) * D:(ec % 2) * D + D],
            in_=d_wqT[ec * 128:(ec + 1) * 128, :]), deps=deps, seg=SG)

    e_wq[0] = load_wq(0, [e_kmm[(NCH - 2, 3)]])
    e_wq[1] = load_wq(1, [e_kmm[(NCH - 1, 3)]])

    def emit_qmm(ec):
        wqslot = wk2[:, (ec % 2) * D:(ec % 2) * D + D]
        for tb in range(NTB):
            for dc in range(NCH):
                e = P.op("pe", lambda e, tb=tb, dc=dc, wqslot=wqslot: e.matmul(
                    pk[:, tb * 512:(tb + 1) * 512],
                    wqslot[:, dc * 128:(dc + 1) * 128],
                    bufA[:, dc * T + tb * 512: dc * T + (tb + 1) * 512],
                    start=(dc == 0), stop=(dc == NCH - 1), skip_group_check=True),
                    deps=[e_wq[ec], e_qev.get((ec - 1, tb)),
                          e_kev[(NCH - 1, tb)] if ec == 0 else None],
                    seg=SG)
            e_qmm[(ec, tb)] = e
            qoff = (ec % 2) * T
            e_qev[(ec, tb)] = P.op("act", lambda e, tb=tb, qoff=qoff: e.activation(
                qh[:, qoff + tb * 512:qoff + (tb + 1) * 512],
                pk[:, tb * 512:(tb + 1) * 512], AF.Copy),
                deps=[e_qmm[(ec, tb)], e_vh.get((ec - 2, tb))], seg=SG)

    def emit_vhat(ec):
        qoff = (ec % 2) * T
        for tb in range(NTB):
            e_vh[(ec, tb)] = P.op("dve", lambda e, ec=ec, tb=tb, qoff=qoff: e.scalar_tensor_tensor(
                bufB[:, ec * T + tb * 512: ec * T + (tb + 1) * 512],
                bufB[:, ec * T + tb * 512: ec * T + (tb + 1) * 512],
                ccols[:, ec:ec + 1],
                qh[:, qoff + tb * 512:qoff + (tb + 1) * 512], ALU.add, ALU.mult),
                deps=[e_qev[(ec, tb)], e_ccev], seg=SG)

    emit_qmm(0)
    e_wq[2] = load_wq(2, [e_qmm[(0, 3)]])
    emit_qmm(1)
    e_wq[3] = load_wq(3, [e_qmm[(1, 3)]])
    for c in range(NCH):
        e_ccmm[c] = P.op("pe", lambda e, c=c: e.matmul(
            pv[:, 32 + c:33 + c], scr[0:8, c * 128:(c + 1) * 128],
            sel1c[:], start=True, stop=True, skip_group_check=True),
            deps=[e_ag1l, e_sel1], seg=SG)
    e_ccev = P.op("dve", lambda e: e.tensor_copy(ccols[:], pv[:, 32:48]),
                  deps=[e_ccmm[NCH - 1]], seg=SG)
    emit_vhat(0)
    emit_vhat(1)
    for ec in range(2, NCH):
        emit_qmm(ec)
        if ec + 2 < NCH:
            e_wq[ec + 2] = load_wq(ec + 2, [e_qmm[(ec, 3)]])
        emit_vhat(ec)

    e_wo = [None] * NCH
    for ec in range(NCH):
        e_wo[ec] = P.dma("gp", lambda e, ec=ec: e.dma_start(
            out=bufA[:, ec * T:(ec + 1) * T],
            in_=d_woT[ec * 128:(ec + 1) * 128, :]),
            deps=[e_qmm[(NCH - 1, 3)]], seg=SG)

    e_xwl = {}
    e_womm = {}
    e_zev = {}
    e_x1 = {}
    e_x1b = {}
    e_x1st = {}
    e_x1bst = {}
    e_sw = {}
    e_rt = {}
    e_sq1 = {}
    e_racc = {}
    prev_ln_tail = None

    for ts in range(NTS):
        tb = ts // 4
        e_xwl[ts] = P.dma("sp", lambda e, ts=ts: e.dma_start(
            out=xw[:], in_=d_xtok[ts * 128:(ts + 1) * 128, :]),
            deps=[e_zev.get((ts - 1, 3)), e_pst[(NCH - 1, 3)] if ts == 0 else None],
            seg=SG)
        for ec in range(NCH):
            for fb in range(NTB):
                e = P.op("pe", lambda e, ts=ts, ec=ec, fb=fb: e.matmul(
                    pk[:, fb * 512:(fb + 1) * 512],
                    bufB[:, ec * T + ts * 128: ec * T + (ts + 1) * 128],
                    bufA[:, ec * T + fb * 512: ec * T + (fb + 1) * 512],
                    start=(ec == 0), stop=(ec == NCH - 1), skip_group_check=True),
                    deps=[e_wo[ec], e_vh[(ec, tb)],
                          e_zev.get((ts - 1, fb)),
                          e_qev[(NCH - 1, fb)] if ts == 0 and ec == 0 else None],
                    seg=SG)
            e_womm[(ts, ec)] = e
        for fb in range(NTB):
            e_zev[(ts, fb)] = P.op("dve", lambda e, fb=fb: e.tensor_add(
                scr[:, fb * 512:(fb + 1) * 512], pk[:, fb * 512:(fb + 1) * 512],
                xw[:, fb * 512:(fb + 1) * 512]),
                deps=[e_womm[(ts, NCH - 1)], e_xwl[ts],
                      e_ccmm[NCH - 1] if ts == 0 else None,
                      e_ag1l if ts == 0 else None],
                seg=SG)
        e_ms = P.op("dve", lambda e: e.reduce_sum(col(C_MS), scr[:], AX.X),
                    deps=[e_zev[(ts, fb)] for fb in range(NTB)], seg=SG)
        e_m = P.op("dve", lambda e: e.tensor_scalar_mul(col(C_M), col(C_MS), col(C_1OD)),
                   deps=[e_ms], seg=SG)
        e_sq1[ts] = P.op("act", lambda e: e.activation(
            pv[:], scr[:], AF.Square, accum_out=col(C_S2)),
            deps=[e_zev[(ts, fb)] for fb in range(NTB)] +
                 [e_ccev if ts == 0 else None],
            seg=SG)
        e_s2m = P.op("dve", lambda e: e.tensor_scalar_mul(col(C_S2M), col(C_S2), col(C_1OD)),
                     deps=[e_sq1[ts]], seg=SG)
        e_m2 = P.op("dve", lambda e: e.tensor_mul(col(C_M2), col(C_M), col(C_M)),
                    deps=[e_m], seg=SG)
        e_var = P.op("dve", lambda e: e.tensor_sub(col(C_VAR), col(C_S2M), col(C_M2)),
                     deps=[e_s2m, e_m2], seg=SG)
        e_std = P.op("act", lambda e: e.activation(col(C_STD), col(C_VAR), AF.Sqrt, bias=col(C_EPS)),
                     deps=[e_var], seg=SG)
        e_rstd = P.op("dve", lambda e: e.reciprocal(col(C_RSTD), col(C_STD)),
                      deps=[e_std], seg=SG)
        e_x1[ts] = P.op("dve", lambda e: e.tensor_scalar(
            x1t[:], scr[:], col(C_M), col(C_RSTD), ALU.subtract, ALU.mult),
            deps=[e_rstd, e_m, e_x1st.get(ts - 1)], seg=SG)
        slot = (ts % 2) * T
        e_x1b[ts] = P.op("dve", lambda e, slot=slot: e.tensor_copy(
            x1b2[:, slot:slot + T], x1t[:]),
            deps=[e_x1[ts], e_x1bst.get(ts - 2)], seg=SG)
        e_x1st[ts] = P.dma("sp", lambda e, ts=ts: e.dma_start(
            out=d_x1f[ts * 128:(ts + 1) * 128, :], in_=x1t[:]),
            deps=[e_x1[ts]], seg=SG)
        e_x1bst[ts] = P.dma("act", lambda e, ts=ts, slot=slot: e.dma_start(
            out=d_x1b[ts * 128:(ts + 1) * 128, :], in_=x1b2[:, slot:slot + T]),
            deps=[e_x1b[ts]], seg=SG)
        for ei in range(8):
            k = (ts * 8 + ei) % 2
            e_sw[(ts, ei)] = P.dma("gp", lambda e, ts=ts, ei=ei, k=k: e.dma_start(
                out=sww[:, k * D:(k + 1) * D],
                in_=d_sw[ei:ei + 1, (ts * 128) * D:(ts * 128 + 128) * D].rearrange(
                    "o (p d) -> (o p) d", p=128)),
                deps=[e_rt.get((ts, ei - 2)) if ei >= 2 else e_rt.get((ts - 1, 6 + ei))],
                seg=SG)
            rc = C_RA if (ts * 8 + ei) % 2 == 0 else C_RB
            e_rt[(ts, ei)] = P.op("dve", lambda e, k=k, slot=slot, rc=rc: e.scalar_tensor_tensor(
                sww[:, k * D:(k + 1) * D], sww[:, k * D:(k + 1) * D], col(C_ZERO),
                x1b2[:, slot:slot + T], ALU.add, ALU.mult, accum_out=col(rc)),
                deps=[e_sw[(ts, ei)], e_x1b[ts]], seg=SG)
            e_racc[(ts, ei)] = P.op("dve", lambda e, ei=ei, rc=rc: e.tensor_add(
                logacc[:, ei:ei + 1], logacc[:, ei:ei + 1], col(rc)),
                deps=[e_rt[(ts, ei)], e_la0 if ts == 0 else None], seg=SG)

    e_lmm = P.op("pe", lambda e: e.matmul(
        pv[0:1, 0:8], col(C_ONE), logacc[:], start=True, stop=True, skip_group_check=True),
        deps=[e_racc[(NTS - 1, 7)], e_sq1[NTS - 1]], seg=SG)
    e_lgr = P.op("dve", lambda e: e.tensor_copy(row(R_LG), pv[0:1, 0:8]), deps=[e_lmm], seg=SG)
    e_ci2 = P.dma("sp", lambda e: e.dma_start(out=d_ci2, in_=row(R_LG)), deps=[e_lgr], seg=SG)

    SG = 2
    e_ag2 = P.op("gp", lambda e: e.collective_compute(
        "AllGather", ALU.bypass, replica_groups=[list(range(NCORES))],
        ins=[d_ci2], outs=[d_co2[0:NCORES, :]]), deps=[e_ci2], kind="cc", seg=SG)
    e_ag2l = P.dma("sp", lambda e: e.dma_start(out=ag2s[:], in_=d_co2),
                   deps=[e_ag2], seg=SG)
    e_smm = P.op("pe", lambda e: e.matmul(
        pv[0:1, 8:16], sel2c[:], ag2s[:], start=True, stop=True, skip_group_check=True),
        deps=[e_ag2l, e_sel2, e_lmm], seg=SG)
    e_l2 = P.op("dve", lambda e: e.tensor_add(row(R_L2), pv[0:1, 8:16], swbrow[:]),
                deps=[e_smm, e_swb, e_lgr], seg=SG)
    e_nm = P.op("dve", lambda e: e.reduce_max(sc(S_NEGMAX), row(R_L2), AX.X, negate=True),
                deps=[e_l2], seg=SG)
    e_ex = P.op("act", lambda e: e.activation(row(R_EXP), row(R_L2), AF.Exp, bias=sc(S_NEGMAX)),
                deps=[e_nm, e_l2], seg=SG)
    e_sm = P.op("dve", lambda e: e.reduce_sum(sc(S_SUM), row(R_EXP), AX.X), deps=[e_ex], seg=SG)
    e_rs = P.op("dve", lambda e: e.reciprocal(sc(S_RSUM), sc(S_SUM)), deps=[e_sm], seg=SG)
    e_pr = P.op("dve", lambda e: e.tensor_scalar_mul(row(R_PR), row(R_EXP), sc(S_RSUM)),
                deps=[e_ex, e_rs], seg=SG)
    e_m1 = P.op("dve", lambda e: e.reduce_max(sc(S_M1), row(R_PR), AX.X), deps=[e_pr], seg=SG)
    e_oh1 = P.op("dve", lambda e: e.tensor_single_scalar(row(R_OH1), row(R_PR), sc(S_M1), ALU.is_ge),
                 deps=[e_m1, e_pr], seg=SG)
    e_t1 = P.op("dve", lambda e: e.tensor_mul(row(R_T1), row(R_OH1), row(R_PR)), deps=[e_oh1], seg=SG)
    e_pm = P.op("dve", lambda e: e.tensor_sub(row(R_PM), row(R_PR), row(R_T1)), deps=[e_t1], seg=SG)
    e_m2_ = P.op("dve", lambda e: e.reduce_max(sc(S_M2), row(R_PM), AX.X), deps=[e_pm], seg=SG)
    e_oh2 = P.op("dve", lambda e: e.tensor_single_scalar(row(R_OH2), row(R_PM), sc(S_M2), ALU.is_ge),
                 deps=[e_m2_, e_pm], seg=SG)
    e_st1 = P.op("pe", lambda e: e.transpose(pv[0:8, 16:17], row(R_OH1), identc[0:1, 0:1]),
                 deps=[e_oh1, e_smm, e_ident], seg=SG)
    e_st2 = P.op("pe", lambda e: e.transpose(pv[0:8, 17:18], row(R_OH2), identc[0:1, 0:1]),
                 deps=[e_oh2], seg=SG)
    e_selT = P.op("dve", lambda e: e.tensor_copy(selT[:], pv[0:8, 16:18]),
                  deps=[e_st1, e_st2], seg=SG)
    e_sb1 = P.op("pe", lambda e: e.matmul(pv[:, 32:40], onesrow[:], row(R_OH1),
                                          start=True, stop=True, skip_group_check=True),
                 deps=[e_oh1, e_or, e_st1, e_selT], seg=SG)
    e_sb2 = P.op("pe", lambda e: e.matmul(pv[:, 40:48], onesrow[:], row(R_OH2),
                                          start=True, stop=True, skip_group_check=True),
                 deps=[e_oh2, e_sb1], seg=SG)
    e_selbc = P.op("dve", lambda e: e.tensor_copy(selbc[:, 0:16], pv[:, 32:48]),
                   deps=[e_sb1, e_sb2], seg=SG)
    e_g1 = P.op("pe", lambda e: e.matmul(pv[:, 48:49], onesrow[:], sc(S_M1),
                                         start=True, stop=True, skip_group_check=True),
                deps=[e_m1, e_sb2, e_selbc], seg=SG)
    e_g2 = P.op("pe", lambda e: e.matmul(pv[:, 49:50], onesrow[:], sc(S_M2),
                                         start=True, stop=True, skip_group_check=True),
                deps=[e_m2_, e_g1], seg=SG)
    e_gbc = P.op("dve", lambda e: e.tensor_copy(gbc[:], pv[:, 48:50]),
                 deps=[e_g1, e_g2], seg=SG)
    e_gath = []
    for k in range(2):
        ek0 = P.op("dve", lambda e, k=k: e.memset(w1fd[:, k * 128:(k + 1) * 128], 0.0), seg=SG)
        ek1 = P.op("dve", lambda e, k=k: e.memset(w1fd[:, (2 + k) * 128:(3 + k) * 128], 0.0), seg=SG)
        last0, last1 = ek0, ek1
        for ei in range(8):
            last0 = P.op("dve", lambda e, k=k, ei=ei: e.scalar_tensor_tensor(
                w1fd[:, k * 128:(k + 1) * 128], w1all[:, ei * 128:(ei + 1) * 128],
                selbc[:, k * 8 + ei:k * 8 + ei + 1], w1fd[:, k * 128:(k + 1) * 128],
                ALU.mult, ALU.add),
                deps=[e_w1all[ei], e_selbc, last0], seg=SG)
            last1 = P.op("dve", lambda e, k=k, ei=ei: e.scalar_tensor_tensor(
                w1fd[:, (2 + k) * 128:(3 + k) * 128], w2all[:, ei * 128:(ei + 1) * 128],
                selbc[:, k * 8 + ei:k * 8 + ei + 1], w1fd[:, (2 + k) * 128:(3 + k) * 128],
                ALU.mult, ALU.add),
                deps=[e_w2all[ei], e_selbc, last1], seg=SG)
        e_gath.append((last0, last1))
    e_wfd_parts = [P.dma("act", lambda e, c=c: e.dma_start(
        out=d_wfd[c * 128:(c + 1) * 128, :],
        in_=w1fd[:, c * 128:(c + 1) * 128]),
        deps=[e_gath[0][0], e_gath[0][1], e_gath[1][0], e_gath[1][1]], seg=SG)
        for c in range(4)]
    e_wfd_st = e_wfd_parts[-1]
    e_wT = []
    for c in range(4):
        e_wT.append(P.dma("act", lambda e, c=c: e.dma_start(
            out=wexp[:, c * 128:(c + 1) * 128],
            in_=d_wfd[c * 128:(c + 1) * 128, :], transpose=True),
            deps=[e_wfd_st], seg=SG))
    e_w2s = []
    for k in range(2):
        e_w2s.append(P.op("dve", lambda e, k=k: e.tensor_scalar_mul(
            wexp[:, (2 + k) * 128:(3 + k) * 128], wexp[:, (2 + k) * 128:(3 + k) * 128],
            gbc[:, k:k + 1]), deps=[e_wT[2 + k], e_gbc], seg=SG))
    e_bg1 = P.op("pe", lambda e: e.matmul(pv[:, 52:54], b1sb[:], selT[:],
                                          start=True, stop=True, skip_group_check=True),
                 deps=[e_selT, e_b1, e_g2, e_gbc], seg=SG)
    e_bg2 = P.op("pe", lambda e: e.matmul(pv[:, 54:56], b2sb[:], selT[:],
                                          start=True, stop=True, skip_group_check=True),
                 deps=[e_selT, e_b2, e_bg1], seg=SG)
    e_b1c = P.op("dve", lambda e: e.tensor_copy(b1cols[:], pv[:, 52:54]), deps=[e_bg1], seg=SG)
    e_b2c = P.op("dve", lambda e: e.tensor_copy(b2cols[:], pv[:, 54:56]), deps=[e_bg2], seg=SG)
    e_bt = P.op("dve", lambda e: e.tensor_scalar_mul(col(C_RA), b2cols[:, 0:1], gbc[:, 0:1]),
                deps=[e_b2c, e_gbc], seg=SG)
    e_bc2 = P.op("dve", lambda e: e.scalar_tensor_tensor(
        bc2col[:], b2cols[:, 1:2], gbc[:, 1:2], col(C_RA), ALU.mult, ALU.add),
        deps=[e_bt], seg=SG)

    e_w3 = [None] * NCH
    for dc in range(NCH):
        e_w3[dc] = P.dma("gp", lambda e, dc=dc: e.dma_start(
            out=bufA[:, dc * T:(dc + 1) * T],
            in_=d_w3T[dc * 128:(dc + 1) * 128, :]),
            deps=[e_womm[(NTS - 1, dc)]], seg=SG)
    e_x1T = [None] * NCH
    for dc in range(NCH):
        e_x1T[dc] = P.dma("act", lambda e, dc=dc: e.dma_start(
            out=bufB[:, dc * T:(dc + 1) * T],
            in_=d_x1b[:, dc * 128:(dc + 1) * 128], transpose=True),
            deps=[e_x1bst[NTS - 1], e_womm[(NTS - 1, dc)], e_wfd_st], seg=SG)

    uslot = x1b2[:, 0:T]
    x1T3d = bufB[:].rearrange("p (dc t) -> p dc t", dc=NCH)
    e_xw3 = {}
    e_hmm = {}
    e_gel = {}
    e_omm = {}
    e_u1 = {}
    e_u = {}
    e_ymm = {}
    e_z2 = {}
    e_sq2 = {}
    e_outw = {}
    e_ost = {}

    for ts in range(NTS):
        e_xw3[ts] = P.dma("sp", lambda e, ts=ts: e.dma_start(
            out=xw[:], in_=d_x1f[ts * 128:(ts + 1) * 128, :]),
            deps=[e_x1st[NTS - 1] if ts == 0 else None,
                  e_zev[(NTS - 1, 3)] if ts == 0 else None,
                  e_z2.get((ts - 1, 3))], seg=SG)
        for k in range(2):
            for h in range(NCH):
                e = P.op("pe", lambda e, k=k, h=h, ts=ts: e.matmul(
                    pv[:, h * 128:(h + 1) * 128],
                    wexp[:, k * 128:(k + 1) * 128],
                    bufB[:, h * T + ts * 128: h * T + (ts + 1) * 128],
                    start=True, stop=True, skip_group_check=True),
                    deps=[e_wT[k], e_x1T[h],
                          (e_bg2 if ts == 0 and k == 0 else None) if h == 0 else None,
                          (e_b2c if ts == 0 and k == 0 else None) if h == 0 else None,
                          (e_gel[(ts, 0)] if k == 1 else None) if h == 0 else None,
                          e_z2.get((ts - 1, h // 4)) if k == 0 else None],
                    seg=SG)
            e_hmm[(ts, k)] = e
            e_gel[(ts, k)] = P.op("act", lambda e, k=k: e.activation(
                qh[:, 0:T], pv[:], AF.Gelu, bias=b1cols[:, k:k + 1]),
                deps=[e_hmm[(ts, k)], e_b1c,
                      e_omm.get((ts, k - 1)),
                      e_u.get(ts - 1) if k == 0 else None],
                seg=SG)
            for h in range(NCH):
                e = P.op("pe", lambda e, k=k, h=h: e.matmul(
                    pk[:, h * 128:(h + 1) * 128],
                    wexp[:, (2 + k) * 128:(3 + k) * 128],
                    qh[:, h * 128:(h + 1) * 128],
                    start=(k == 0 and h % 4 == 0),
                    stop=(k == 1 and h % 4 == 3), skip_group_check=True),
                    deps=[e_gel[(ts, k)], e_w2s[k],
                          e_sq2.get(ts - 1) if k == 0 and h == 0 else None],
                    seg=SG)
            e_omm[(ts, k)] = e
        e_u1[ts] = P.op("dve", lambda e: e.tensor_scalar_add(qh[:, 0:T], pk[:], bc2col[:]),
                        deps=[e_omm[(ts, 1)], e_bc2, e_gel[(ts, 1)]], seg=SG)
        e_u[ts] = P.op("dve", lambda e, ts=ts: e.tensor_add(
            uslot[:], qh[:, 0:T], x1T3d[:, :, ts * 128:(ts + 1) * 128]),
            deps=[e_u1[ts], e_ymm.get((ts - 1, 3)),
                  e_x1bst[NTS - 2] if ts == 0 else None] +
                 ([d for d in e_x1T] if ts == 0 else []), seg=SG)
        for fb in range(NTB):
            for dc in range(NCH):
                e = P.op("pe", lambda e, fb=fb, dc=dc: e.matmul(
                    pv[:, fb * 512:(fb + 1) * 512],
                    uslot[:, dc * 128:(dc + 1) * 128],
                    bufA[:, dc * T + fb * 512: dc * T + (fb + 1) * 512],
                    start=(dc == 0), stop=False, skip_group_check=True),
                    deps=[e_u[ts], e_w3[dc], e_gel[(ts, 1)],
                          e_z2.get((ts - 1, fb))], seg=SG)
            e_ymm[(ts, fb)] = P.op("pe", lambda e, fb=fb: e.matmul(
                pv[:, fb * 512:(fb + 1) * 512], onesb3[:],
                b3row[0:1, fb * 512:(fb + 1) * 512],
                start=False, stop=True, skip_group_check=True),
                deps=[e, e_b3, e_ob3], seg=SG)
        for fb in range(NTB):
            e_z2[(ts, fb)] = P.op("dve", lambda e, fb=fb: e.tensor_add(
                scr[:, fb * 512:(fb + 1) * 512], pv[:, fb * 512:(fb + 1) * 512],
                xw[:, fb * 512:(fb + 1) * 512]),
                deps=[e_ymm[(ts, fb)], e_xw3[ts],
                      e_lgr if ts == 0 and fb == 0 else None], seg=SG)
        e_ms2 = P.op("dve", lambda e: e.reduce_sum(col(C_MS), scr[:], AX.X),
                     deps=[e_z2[(ts, fb)] for fb in range(NTB)], seg=SG)
        e_mm2 = P.op("dve", lambda e: e.tensor_scalar_mul(col(C_M), col(C_MS), col(C_1OD)),
                     deps=[e_ms2], seg=SG)
        e_sq2[ts] = P.op("act", lambda e: e.activation(
            pk[:], scr[:], AF.Square, accum_out=col(C_S2)),
            deps=[e_z2[(ts, fb)] for fb in range(NTB)] + [e_u1[ts]], seg=SG)
        e_s2m2 = P.op("dve", lambda e: e.tensor_scalar_mul(col(C_S2M), col(C_S2), col(C_1OD)),
                      deps=[e_sq2[ts]], seg=SG)
        e_mq2 = P.op("dve", lambda e: e.tensor_mul(col(C_M2), col(C_M), col(C_M)),
                     deps=[e_mm2], seg=SG)
        e_var2 = P.op("dve", lambda e: e.tensor_sub(col(C_VAR), col(C_S2M), col(C_M2)),
                      deps=[e_s2m2, e_mq2], seg=SG)
        e_std2 = P.op("act", lambda e: e.activation(col(C_STD), col(C_VAR), AF.Sqrt, bias=col(C_EPS)),
                      deps=[e_var2], seg=SG)
        e_rstd2 = P.op("dve", lambda e: e.reciprocal(col(C_RSTD), col(C_STD)),
                       deps=[e_std2], seg=SG)
        e_outw[ts] = P.op("dve", lambda e: e.tensor_scalar(
            x1t[:], scr[:], col(C_M), col(C_RSTD), ALU.subtract, ALU.mult),
            deps=[e_rstd2, e_mm2, e_ost.get(ts - 1), e_x1st[NTS - 1] if ts == 0 else None],
            seg=SG)
        e_ost[ts] = P.dma("sp", lambda e, ts=ts: e.dma_start(
            out=d_out[ts * 128:(ts + 1) * 128, :], in_=x1t[:]),
            deps=[e_outw[ts]], seg=SG)

    P.op("sp", lambda e: e.nop(), deps=[e_ost[NTS - 1]], seg=SG)

    P.emit(nc, block, sems)

    for h in reversed(ctx):
        h.__exit__(None, None, None)
    return nc


_NC_CACHE = None


def kernel(L0_attn_x, L0_attn_Wq, L0_attn_Wk, L0_attn_Wv, L0_attn_Wo,
           L0_ffn_switch_w, L0_ffn_switch_b, L0_ffn_w1, L0_ffn_b1,
           L0_ffn_w2, L0_ffn_b2, L0_ffn_w3, L0_ffn_b3):
    global _NC_CACHE
    f32 = np.float32
    x = np.ascontiguousarray(np.asarray(L0_attn_x, dtype=f32)).reshape(NCORES, T, D)
    xfeat = np.ascontiguousarray(np.swapaxes(x, 1, 2))
    def slab(w):
        w4 = np.asarray(w, dtype=f32).reshape(16, 128, 16, 128)
        return np.ascontiguousarray(w4.transpose(0, 3, 2, 1).reshape(2048, 2048))
    wqT = slab(L0_attn_Wq)
    wkT = slab(L0_attn_Wk)
    wvT = slab(L0_attn_Wv)
    woT = np.ascontiguousarray(np.asarray(L0_attn_Wo, dtype=f32).T)
    w3T = np.ascontiguousarray(np.asarray(L0_ffn_w3, dtype=f32).T)
    sw = np.ascontiguousarray(np.asarray(L0_ffn_switch_w, dtype=f32))
    swb = np.asarray(L0_ffn_switch_b, dtype=f32).reshape(1, 8)
    w1 = np.ascontiguousarray(np.asarray(L0_ffn_w1, dtype=f32).reshape(8 * 128, 128))
    w2 = np.ascontiguousarray(np.asarray(L0_ffn_w2, dtype=f32).reshape(8 * 128, 128))
    b1 = np.ascontiguousarray(np.asarray(L0_ffn_b1, dtype=f32))
    b2 = np.ascontiguousarray(np.asarray(L0_ffn_b2, dtype=f32))
    b3 = np.asarray(L0_ffn_b3, dtype=f32).reshape(1, D)

    if _NC_CACHE is None:
        _NC_CACHE = build_nc()
    nc = _NC_CACHE

    in_maps = []
    for c in range(NCORES):
        sel1 = np.zeros((8, 1), dtype=f32)
        if c % 2 == 1:
            sel1[c - 1, 0] = 1.0
        sel2 = np.zeros((8, 1), dtype=f32)
        sel2[(c // 2) * 2, 0] = 1.0
        in_maps.append({
            "xtok": x[c], "xfeat": xfeat[c],
            "wqT": wqT, "wkT": wkT, "wvT": wvT, "woT": woT, "w3T": w3T,
            "sw": sw, "swb": swb, "w1": w1, "w2": w2, "b1": b1, "b2": b2,
            "b3": b3, "sel1": sel1, "sel2": sel2,
        })
    res = run_bass_kernel_spmd(nc, in_maps, core_ids=list(range(NCORES)))
    out = np.stack([np.asarray(res.results[c]["out"]) for c in range(NCORES)], axis=0)
    return out.reshape(4, 4096, D).astype(np.float32)


# version 07 (final)
# speedup vs baseline: 1.0256x; 1.0195x over previous
import math
import numpy as np
import ml_dtypes
import concourse.bass as bass
import concourse.mybir as mybir
from concourse.bass_utils import run_bass_kernel_spmd

F32 = mybir.dt.float32
BF16 = mybir.dt.bfloat16
AF = mybir.ActivationFunctionType
ALU = mybir.AluOpType
AX = mybir.AxisListType

NCORES = 8
D = 2048
T = 2048
NCH = 16
NTS = 16
NTB = 4
DENOM = math.sqrt(2048.0) / 4096.0
EPS = 1e-5

ENGMAP = {"pe": "tensor", "dve": "vector", "act": "scalar", "gp": "gpsimd", "sp": "sync"}


class Prog:

    def __init__(self):
        self.ops = []

    def op(self, eng, fn, deps=(), kind="c", seg=0):
        i = len(self.ops)
        self.ops.append({
            "eng": eng, "fn": fn, "kind": kind, "seg": seg,
            "deps": [d for d in deps if d is not None],
        })
        return i

    def dma(self, eng, fn, deps=(), seg=0):
        return self.op(eng, fn, deps, kind="d", seg=seg)

    def emit(self, nc, block, sems):
        ops = self.ops
        def semkey(o):
            if o["kind"] == "cc":
                return "cc"
            if o["kind"] == "d":
                return o["eng"] + "q"
            return o["eng"]
        targets = set()
        for o in ops:
            for d in o["deps"]:
                targets.add(d)
        counts = {}
        for i, o in enumerate(ops):
            o["inc"] = 0
            if i in targets or o["kind"] in ("d", "cc"):
                k = semkey(o)
                step = 16 if o["kind"] == "d" else 1
                counts[k] = counts.get(k, 0) + step
                o["inc"] = step
                o["semk"] = k
                o["val"] = counts[k]
        nsegs = 1 + max(o["seg"] for o in ops)
        for seg in range(nsegs):
            for eng in ("gp", "sp", "act", "dve", "pe"):
                mine = [o for o in ops if o["seg"] == seg and o["eng"] == eng]
                if not mine:
                    continue
                def body(e, mine=mine, eng=eng):
                    marks = {}
                    for o in mine:
                        for d in o["deps"]:
                            p = ops[d]
                            if p["inc"] == 0:
                                raise RuntimeError("dep on non-target op")
                            k, v = p["semk"], p["val"]
                            if marks.get(k, -1) >= v:
                                continue
                            e.wait_ge(sems[k], v)
                            marks[k] = v
                        inst = o["fn"](e)
                        if o["inc"]:
                            inst.then_inc(sems[o["semk"]], o["inc"])
                getattr(block, ENGMAP[eng])(body)


def build_nc():
    nc = bass.Bass("TRN2", target_bir_lowering=False)

    d_xtok = nc.dram_tensor("xtok", [T, D], F32, kind="ExternalInput").ap()
    d_xfeat = nc.dram_tensor("xfeat", [D, T], F32, kind="ExternalInput").ap()
    d_wqT = nc.dram_tensor("wqT", [D, D], F32, kind="ExternalInput").ap()
    d_wkT = nc.dram_tensor("wkT", [D, D], F32, kind="ExternalInput").ap()
    d_wvT = nc.dram_tensor("wvT", [D, D], F32, kind="ExternalInput").ap()
    d_woT = nc.dram_tensor("woT", [D, D], F32, kind="ExternalInput").ap()
    d_w3T = nc.dram_tensor("w3T", [D, D], F32, kind="ExternalInput").ap()
    d_sw = nc.dram_tensor("sw", [8, T * D], F32, kind="ExternalInput").ap()
    d_swb = nc.dram_tensor("swb", [1, 8], F32, kind="ExternalInput").ap()
    d_w1 = nc.dram_tensor("w1", [8 * 128, 128], F32, kind="ExternalInput").ap()
    d_w2 = nc.dram_tensor("w2", [8 * 128, 128], F32, kind="ExternalInput").ap()
    d_b1 = nc.dram_tensor("b1", [8, 128], F32, kind="ExternalInput").ap()
    d_b2 = nc.dram_tensor("b2", [8, 128], F32, kind="ExternalInput").ap()
    d_b3 = nc.dram_tensor("b3", [1, D], F32, kind="ExternalInput").ap()
    d_sel1 = nc.dram_tensor("sel1", [8, 1], F32, kind="ExternalInput").ap()
    d_sel2 = nc.dram_tensor("sel2", [8, 1], F32, kind="ExternalInput").ap()
    d_out = nc.dram_tensor("out", [T, D], F32, kind="ExternalOutput").ap()

    d_x1f = nc.dram_tensor("x1f", [T, D], F32, kind="Internal").ap()
    d_x1b = nc.dram_tensor("x1b", [T, D], BF16, kind="Internal").ap()
    d_wfd = nc.dram_tensor("wfd", [4 * 128, 128], BF16, kind="Internal").ap()
    d_ci1 = nc.dram_tensor("ci1", [1, D], F32, kind="Internal").ap()
    d_co1 = nc.dram_tensor("co1", [8, D], F32, kind="Internal").ap()
    d_ci2 = nc.dram_tensor("ci2", [1, 8], F32, kind="Internal").ap()
    d_co2 = nc.dram_tensor("co2", [8, 8], F32, kind="Internal").ap()

    ctx = []
    def sb(name, shape, dt):
        h = nc.sbuf_tensor(name, shape, dt)
        t = h.__enter__(); ctx.append(h); return t
    def psum(name, shape, dt=F32):
        h = nc.psum_tensor(name, shape, dt)
        t = h.__enter__(); ctx.append(h); return t

    bufA = sb("bufA", [128, NCH * T], BF16)
    bufB = sb("bufB", [128, NCH * T], BF16)
    wk2 = sb("wk2", [128, 2 * D], BF16)
    wv2 = sb("wv2", [128, 2 * D], BF16)
    sww = sb("sww", [128, 2 * D], BF16)
    qh = sb("qh", [128, 2 * T], BF16)
    x1b2 = sb("x1b2", [128, 2 * T], BF16)
    wexp = sb("wexp", [128, 4 * 128], BF16)
    w1all = sb("w1all", [128, 8 * 128], BF16)
    w2all = sb("w2all", [128, 8 * 128], BF16)
    w1fd = sb("w1fd", [128, 4 * 128], BF16)

    xw = sb("xw", [128, T], F32)
    scr = sb("scr", [128, T], F32)
    x1t = sb("x1t", [128, T], F32)
    b3row = sb("b3row", [1, D], BF16)
    onesb3 = sb("onesb3", [1, 128], BF16)
    identc = sb("identc", [128, 1], F32)
    ccols = sb("ccols", [128, 16], F32)
    ag2s = sb("ag2s", [8, 8], F32)
    selT = sb("selT", [8, 2], F32)
    selbc = sb("selbc", [128, 16], F32)
    b1sb = sb("b1sb", [8, 128], F32)
    b2sb = sb("b2sb", [8, 128], F32)
    b1cols = sb("b1cols", [128, 2], F32)
    b2cols = sb("b2cols", [128, 2], F32)
    bc2col = sb("bc2col", [128, 1], F32)
    gbc = sb("gbc", [128, 2], F32)
    logacc = sb("logacc", [128, 8], F32)
    cols = sb("cols", [128, 24], F32)
    (C_DEN, C_ZERO, C_EPS, C_1OD, C_M, C_RSTD, C_S2, C_VAR, C_MS, C_STD, C_M2,
     C_S2M, C_RA, C_RB, C_ONE, C_NH, C_15, C_WE, C_Y, C_T, C_MAGIC, C_SH1) = range(22)
    rows = sb("rows", [1, 64], F32)
    R_LG, R_L2, R_EXP, R_PR, R_OH1, R_OH2, R_PM, R_T1 = range(8)
    s11 = sb("s11", [1, 8], F32)
    S_NEGMAX, S_SUM, S_RSUM, S_M1, S_M2 = range(5)
    onesrow = sb("onesrow", [1, 128], F32)
    swbrow = sb("swbrow", [1, 8], F32)
    sel1c = sb("sel1c", [8, 1], F32)
    sel2c = sb("sel2c", [8, 1], F32)

    pk = psum("pk", [128, T], F32)
    pv = psum("pv", [128, T], F32)

    sems = {}
    for name in ("pe", "dve", "act", "gpq", "spq", "actq", "cc"):
        h = nc.semaphore(name)
        sems[name] = h.__enter__(); ctx.append(h)

    block_h = nc.Block(); block = block_h.__enter__(); ctx.append(block_h)

    P = Prog()
    def col(i):
        return cols[:, i:i + 1]
    def row(i):
        return rows[:, 8 * i:8 * i + 8]
    def sc(i):
        return s11[:, i:i + 1]

    e_ident = P.op("dve", lambda e: e.memset(identc[:], 1.0))
    e_b3 = P.dma("gp", lambda e: e.dma_start(out=b3row[:], in_=d_b3))
    e_ob3 = P.op("dve", lambda e: e.memset(onesb3[:], 1.0))
    e_swb = P.dma("sp", lambda e: e.dma_start(out=swbrow[:], in_=d_swb))
    e_sel1 = P.dma("sp", lambda e: e.dma_start(out=sel1c[:], in_=d_sel1))
    e_sel2 = P.dma("sp", lambda e: e.dma_start(out=sel2c[:], in_=d_sel2))
    e_b1 = P.dma("sp", lambda e: e.dma_start(out=b1sb[:], in_=d_b1))
    e_b2 = P.dma("sp", lambda e: e.dma_start(out=b2sb[:], in_=d_b2))
    e_w1all = [P.dma("gp", lambda e, ei=ei: e.dma_start(
        out=w1all[:, ei * 128:(ei + 1) * 128],
        in_=d_w1[ei * 128:(ei + 1) * 128, :])) for ei in range(8)]
    e_w2all = [P.dma("gp", lambda e, ei=ei: e.dma_start(
        out=w2all[:, ei * 128:(ei + 1) * 128],
        in_=d_w2[ei * 128:(ei + 1) * 128, :])) for ei in range(8)]
    e_cz = P.op("dve", lambda e: e.memset(cols[:], 0.0))
    e_cz = P.op("dve", lambda e: e.memset(col(C_DEN), DENOM), deps=[e_cz])
    e_cz = P.op("dve", lambda e: e.memset(col(C_EPS), EPS), deps=[e_cz])
    e_cz = P.op("dve", lambda e: e.memset(col(C_1OD), 1.0 / D), deps=[e_cz])
    e_cz = P.op("dve", lambda e: e.memset(col(C_ONE), 1.0), deps=[e_cz])
    e_cz = P.op("dve", lambda e: e.memset(col(C_NH), -0.5), deps=[e_cz])
    e_cz = P.op("dve", lambda e: e.memset(col(C_15), 1.5), deps=[e_cz])
    e_cz = P.op("dve", lambda e: e.memset(col(C_MAGIC).bitcast(mybir.dt.int32), 0x5f3759df), deps=[e_cz])
    e_cz = P.op("dve", lambda e: e.memset(col(C_SH1).bitcast(mybir.dt.int32), 1), deps=[e_cz])
    e_la0 = P.op("dve", lambda e: e.memset(logacc[:], 0.0))
    e_or = P.op("dve", lambda e: e.memset(onesrow[:], 1.0))

    e_xT = []
    for dc in range(NCH):
        e_xT.append(P.dma("gp", lambda e, dc=dc: e.dma_start(
            out=bufA[:, dc * T:(dc + 1) * T],
            in_=d_xfeat[dc * 128:(dc + 1) * 128, :])))

    def wchunk_load(dst2, dsrc, ec, deps):
        slot = dst2[:, (ec % 2) * D:(ec % 2) * D + D]
        return P.dma("gp", lambda e: e.dma_start(
            out=slot, in_=dsrc[ec * 128:(ec + 1) * 128, :]), deps=deps)

    e_wk = [None] * NCH
    e_wv = [None] * NCH
    e_kmm = {}
    e_vmm = {}
    e_kev = {}
    e_pst = {}
    e_scan = [None] * NCH
    e_kvtc = [None] * NCH
    e_ccp = [None] * NCH

    for ec in range(2):
        e_wk[ec] = wchunk_load(wk2, d_wkT, ec, [])
        e_wv[ec] = wchunk_load(wv2, d_wvT, ec, [])

    for ec in range(NCH):
        wkslot = wk2[:, (ec % 2) * D:(ec % 2) * D + D]
        wvslot = wv2[:, (ec % 2) * D:(ec % 2) * D + D]
        for dc in range(NCH):
            for tb in range(NTB):
                e = P.op("pe", lambda e, tb=tb, dc=dc, wkslot=wkslot: e.matmul(
                    pk[:, tb * 512:(tb + 1) * 512],
                    wkslot[:, dc * 128:(dc + 1) * 128],
                    bufA[:, dc * T + tb * 512: dc * T + (tb + 1) * 512],
                    start=(dc == 0), stop=(dc == NCH - 1), skip_group_check=True),
                    deps=[e_wk[ec], e_xT[dc],
                          e_kev.get((ec - 1, tb)) if dc == 0 else None])
                if dc == NCH - 1:
                    e_kmm[(ec, tb)] = e
        for tb in range(NTB):
            e_kev[(ec, tb)] = P.op("act", lambda e, tb=tb: e.activation(
                xw[:, tb * 512:(tb + 1) * 512], pk[:, tb * 512:(tb + 1) * 512], AF.Copy),
                deps=[e_kmm[(ec, tb)], e_pst.get((ec - 1, tb))])
        for dc in range(NCH):
            for tb in range(NTB):
                e = P.op("pe", lambda e, tb=tb, dc=dc, wvslot=wvslot: e.matmul(
                    pv[:, tb * 512:(tb + 1) * 512],
                    wvslot[:, dc * 128:(dc + 1) * 128],
                    bufA[:, dc * T + tb * 512: dc * T + (tb + 1) * 512],
                    start=(dc == 0), stop=(dc == NCH - 1), skip_group_check=True),
                    deps=[e_wv[ec], e_xT[dc],
                          e_pst.get((ec - 1, tb)) if dc == 0 else None])
                if dc == NCH - 1:
                    e_vmm[(ec, tb)] = e
        if ec + 2 < NCH:
            e_wk[ec + 2] = wchunk_load(wk2, d_wkT, ec + 2, [e_kmm[(ec, 3)]])
            e_wv[ec + 2] = wchunk_load(wv2, d_wvT, ec + 2, [e_vmm[(ec, 3)]])
        for tb in range(NTB):
            e_pst[(ec, tb)] = P.op("dve", lambda e, tb=tb: e.scalar_tensor_tensor(
                scr[:, tb * 512:(tb + 1) * 512], xw[:, tb * 512:(tb + 1) * 512],
                col(C_DEN), pv[:, tb * 512:(tb + 1) * 512], ALU.mult, ALU.mult),
                deps=[e_kev[(ec, tb)], e_vmm[(ec, tb)],
                      e_scan[ec - 1] if ec else None,
                      e_kvtc[ec - 1] if ec else None,
                      e_ccp[ec - 1] if ec else None])
        e_scan[ec] = P.op("dve", lambda e: e.tensor_tensor_scan(
            scr[:], scr[:], scr[:], 0.0, ALU.add, ALU.bypass),
            deps=[e_pst[(ec, tb)] for tb in range(NTB)])
        e_ccp[ec] = P.dma("sp", lambda e, ec=ec: e.dma_start(
            out=d_ci1[0:1, ec * 128:(ec + 1) * 128].rearrange("o (p u) -> (o p) u", u=1),
            in_=scr[:, T - 1:T]), deps=[e_scan[ec]])
        e_kvtc[ec] = P.op("dve", lambda e, ec=ec: e.tensor_copy(
            bufB[:, ec * T:(ec + 1) * T], scr[:]), deps=[e_scan[ec]])

    e_ci1 = e_ccp[NCH - 1]

    SG = 1
    e_ag1 = P.op("gp", lambda e: e.collective_compute(
        "AllGather", ALU.bypass, replica_groups=[list(range(NCORES))],
        ins=[d_ci1], outs=[d_co1[0:NCORES, :]]), deps=[e_ci1], kind="cc", seg=SG)
    e_ag1l = P.dma("sp", lambda e: e.dma_start(out=scr[0:8, :], in_=d_co1),
                   deps=[e_ag1, e_kvtc[NCH - 1]], seg=SG)

    e_wq = [None] * NCH
    e_qmm = {}
    e_qev = {}
    e_vh = {}
    e_ccmm = [None] * NCH
    e_ccev = None

    def load_wq(ec, deps):
        return P.dma("gp", lambda e, ec=ec: e.dma_start(
            out=wk2[:, (ec % 2) * D:(ec % 2) * D + D],
            in_=d_wqT[ec * 128:(ec + 1) * 128, :]), deps=deps, seg=SG)

    e_wq[0] = load_wq(0, [e_kmm[(NCH - 2, 3)]])
    e_wq[1] = load_wq(1, [e_kmm[(NCH - 1, 3)]])

    def emit_qmm(ec):
        wqslot = wk2[:, (ec % 2) * D:(ec % 2) * D + D]
        for dc in range(NCH):
            for tb in range(NTB):
                e = P.op("pe", lambda e, tb=tb, dc=dc, wqslot=wqslot: e.matmul(
                    pk[:, tb * 512:(tb + 1) * 512],
                    wqslot[:, dc * 128:(dc + 1) * 128],
                    bufA[:, dc * T + tb * 512: dc * T + (tb + 1) * 512],
                    start=(dc == 0), stop=(dc == NCH - 1), skip_group_check=True),
                    deps=[e_wq[ec],
                          (e_qev.get((ec - 1, tb)) if dc == 0 else None),
                          (e_kev[(NCH - 1, tb)] if ec == 0 and dc == 0 else None)],
                    seg=SG)
                if dc == NCH - 1:
                    e_qmm[(ec, tb)] = e
        qoff = (ec % 2) * T
        for tb in range(NTB):
            e_qev[(ec, tb)] = P.op("act", lambda e, tb=tb, qoff=qoff: e.activation(
                qh[:, qoff + tb * 512:qoff + (tb + 1) * 512],
                pk[:, tb * 512:(tb + 1) * 512], AF.Copy),
                deps=[e_qmm[(ec, tb)], e_vh.get((ec - 2, tb))], seg=SG)

    def emit_vhat(ec):
        qoff = (ec % 2) * T
        for tb in range(NTB):
            e_vh[(ec, tb)] = P.op("dve", lambda e, ec=ec, tb=tb, qoff=qoff: e.scalar_tensor_tensor(
                bufB[:, ec * T + tb * 512: ec * T + (tb + 1) * 512],
                bufB[:, ec * T + tb * 512: ec * T + (tb + 1) * 512],
                ccols[:, ec:ec + 1],
                qh[:, qoff + tb * 512:qoff + (tb + 1) * 512], ALU.add, ALU.mult),
                deps=[e_qev[(ec, tb)], e_ccev], seg=SG)

    emit_qmm(0)
    e_wq[2] = load_wq(2, [e_qmm[(0, 3)]])
    emit_qmm(1)
    e_wq[3] = load_wq(3, [e_qmm[(1, 3)]])
    for c in range(NCH):
        e_ccmm[c] = P.op("pe", lambda e, c=c: e.matmul(
            pv[:, 32 + c:33 + c], scr[0:8, c * 128:(c + 1) * 128],
            sel1c[:], start=True, stop=True, skip_group_check=True),
            deps=[e_ag1l, e_sel1], seg=SG)
    e_ccev = P.op("dve", lambda e: e.tensor_copy(ccols[:], pv[:, 32:48]),
                  deps=[e_ccmm[NCH - 1]], seg=SG)
    emit_vhat(0)
    emit_vhat(1)
    for ec in range(2, NCH):
        emit_qmm(ec)
        if ec + 2 < NCH:
            e_wq[ec + 2] = load_wq(ec + 2, [e_qmm[(ec, 3)]])
        emit_vhat(ec)

    e_wo = [None] * NCH
    for ec in range(NCH):
        e_wo[ec] = P.dma("gp", lambda e, ec=ec: e.dma_start(
            out=bufA[:, ec * T:(ec + 1) * T],
            in_=d_woT[ec * 128:(ec + 1) * 128, :]),
            deps=[e_qmm[(NCH - 1, 3)]], seg=SG)

    e_xwl = {}
    e_womm = {}
    e_zev = {}
    e_x1 = {}
    e_x1b = {}
    e_x1st = {}
    e_x1bst = {}
    e_sw = {}
    e_rt = {}
    e_sq1 = {}
    e_racc = {}
    prev_ln_tail = None

    for ts in range(NTS):
        tb = ts // 4
        e_xwl[ts] = P.dma("sp", lambda e, ts=ts: e.dma_start(
            out=xw[:], in_=d_xtok[ts * 128:(ts + 1) * 128, :]),
            deps=[e_zev.get((ts - 1, 3)), e_pst[(NCH - 1, 3)] if ts == 0 else None],
            seg=SG)
        for ec in range(NCH):
            for fb in range(NTB):
                e = P.op("pe", lambda e, ts=ts, ec=ec, fb=fb: e.matmul(
                    pk[:, fb * 512:(fb + 1) * 512],
                    bufB[:, ec * T + ts * 128: ec * T + (ts + 1) * 128],
                    bufA[:, ec * T + fb * 512: ec * T + (fb + 1) * 512],
                    start=(ec == 0), stop=(ec == NCH - 1), skip_group_check=True),
                    deps=[e_wo[ec], e_vh[(ec, tb)],
                          e_zev.get((ts - 1, fb)),
                          e_qev[(NCH - 1, fb)] if ts == 0 and ec == 0 else None],
                    seg=SG)
            e_womm[(ts, ec)] = e
        for fb in range(NTB):
            e_zev[(ts, fb)] = P.op("dve", lambda e, fb=fb: e.tensor_add(
                scr[:, fb * 512:(fb + 1) * 512], pk[:, fb * 512:(fb + 1) * 512],
                xw[:, fb * 512:(fb + 1) * 512]),
                deps=[e_womm[(ts, NCH - 1)], e_xwl[ts],
                      e_ccmm[NCH - 1] if ts == 0 else None,
                      e_ag1l if ts == 0 else None],
                seg=SG)
        e_ms = P.op("dve", lambda e: e.reduce_sum(col(C_MS), scr[:], AX.X),
                    deps=[e_zev[(ts, fb)] for fb in range(NTB)], seg=SG)
        e_m = P.op("dve", lambda e: e.tensor_scalar_mul(col(C_M), col(C_MS), col(C_1OD)),
                   deps=[e_ms], seg=SG)
        e_sq1[ts] = P.op("act", lambda e: e.activation(
            pv[:], scr[:], AF.Square, accum_out=col(C_S2)),
            deps=[e_zev[(ts, fb)] for fb in range(NTB)] +
                 [e_ccev if ts == 0 else None],
            seg=SG)
        e_s2m = P.op("dve", lambda e: e.tensor_scalar_mul(col(C_S2M), col(C_S2), col(C_1OD)),
                     deps=[e_sq1[ts]], seg=SG)
        e_m2 = P.op("dve", lambda e: e.tensor_mul(col(C_M2), col(C_M), col(C_M)),
                    deps=[e_m], seg=SG)
        e_var = P.op("dve", lambda e: e.tensor_sub(col(C_VAR), col(C_S2M), col(C_M2)),
                     deps=[e_s2m, e_m2], seg=SG)
        e_std = P.op("act", lambda e: e.activation(col(C_STD), col(C_VAR), AF.Sqrt, bias=col(C_EPS)),
                     deps=[e_var], seg=SG)
        e_rstd = P.op("dve", lambda e: e.reciprocal(col(C_RSTD), col(C_STD)),
                      deps=[e_std], seg=SG)
        e_x1[ts] = P.op("dve", lambda e: e.tensor_scalar(
            x1t[:], scr[:], col(C_M), col(C_RSTD), ALU.subtract, ALU.mult),
            deps=[e_rstd, e_m, e_x1st.get(ts - 1)], seg=SG)
        slot = (ts % 2) * T
        e_x1b[ts] = P.op("dve", lambda e, slot=slot: e.tensor_copy(
            x1b2[:, slot:slot + T], x1t[:]),
            deps=[e_x1[ts], e_x1bst.get(ts - 2)], seg=SG)
        e_x1st[ts] = P.dma("sp", lambda e, ts=ts: e.dma_start(
            out=d_x1f[ts * 128:(ts + 1) * 128, :], in_=x1t[:]),
            deps=[e_x1[ts]], seg=SG)
        e_x1bst[ts] = P.dma("act", lambda e, ts=ts, slot=slot: e.dma_start(
            out=d_x1b[ts * 128:(ts + 1) * 128, :], in_=x1b2[:, slot:slot + T]),
            deps=[e_x1b[ts]], seg=SG)
        for ei in range(8):
            k = (ts * 8 + ei) % 2
            e_sw[(ts, ei)] = P.dma("gp", lambda e, ts=ts, ei=ei, k=k: e.dma_start(
                out=sww[:, k * D:(k + 1) * D],
                in_=d_sw[ei:ei + 1, (ts * 128) * D:(ts * 128 + 128) * D].rearrange(
                    "o (p d) -> (o p) d", p=128)),
                deps=[e_rt.get((ts, ei - 2)) if ei >= 2 else e_rt.get((ts - 1, 6 + ei))],
                seg=SG)
            rc = C_RA if (ts * 8 + ei) % 2 == 0 else C_RB
            e_rt[(ts, ei)] = P.op("dve", lambda e, k=k, slot=slot, rc=rc: e.scalar_tensor_tensor(
                sww[:, k * D:(k + 1) * D], sww[:, k * D:(k + 1) * D], col(C_ZERO),
                x1b2[:, slot:slot + T], ALU.add, ALU.mult, accum_out=col(rc)),
                deps=[e_sw[(ts, ei)], e_x1b[ts]], seg=SG)
            e_racc[(ts, ei)] = P.op("dve", lambda e, ei=ei, rc=rc: e.tensor_add(
                logacc[:, ei:ei + 1], logacc[:, ei:ei + 1], col(rc)),
                deps=[e_rt[(ts, ei)], e_la0 if ts == 0 else None], seg=SG)

    e_w3 = [None] * NCH
    for dc in range(NCH):
        e_w3[dc] = P.dma("gp", lambda e, dc=dc: e.dma_start(
            out=bufA[:, dc * T:(dc + 1) * T],
            in_=d_w3T[dc * 128:(dc + 1) * 128, :]),
            deps=[e_womm[(NTS - 1, dc)]], seg=SG)
    e_x1T = [None] * NCH
    for dc in range(NCH):
        e_x1T[dc] = P.dma("act", lambda e, dc=dc: e.dma_start(
            out=bufB[:, dc * T:(dc + 1) * T],
            in_=d_x1b[:, dc * 128:(dc + 1) * 128], transpose=True),
            deps=[e_x1bst[NTS - 1], e_womm[(NTS - 1, dc)]], seg=SG)
    e_lmm = P.op("pe", lambda e: e.matmul(
        pv[0:1, 0:8], col(C_ONE), logacc[:], start=True, stop=True, skip_group_check=True),
        deps=[e_racc[(NTS - 1, 7)], e_sq1[NTS - 1]], seg=SG)
    e_lgr = P.op("dve", lambda e: e.tensor_copy(row(R_LG), pv[0:1, 0:8]), deps=[e_lmm], seg=SG)
    e_ci2 = P.dma("sp", lambda e: e.dma_start(out=d_ci2, in_=row(R_LG)), deps=[e_lgr], seg=SG)

    SG = 2
    e_ag2 = P.op("gp", lambda e: e.collective_compute(
        "AllGather", ALU.bypass, replica_groups=[list(range(NCORES))],
        ins=[d_ci2], outs=[d_co2[0:NCORES, :]]), deps=[e_ci2], kind="cc", seg=SG)
    e_ag2l = P.dma("sp", lambda e: e.dma_start(out=ag2s[:], in_=d_co2),
                   deps=[e_ag2], seg=SG)
    e_smm = P.op("pe", lambda e: e.matmul(
        pv[0:1, 8:16], sel2c[:], ag2s[:], start=True, stop=True, skip_group_check=True),
        deps=[e_ag2l, e_sel2, e_lmm], seg=SG)
    e_l2 = P.op("dve", lambda e: e.tensor_add(row(R_L2), pv[0:1, 8:16], swbrow[:]),
                deps=[e_smm, e_swb, e_lgr], seg=SG)
    e_nm = P.op("dve", lambda e: e.reduce_max(sc(S_NEGMAX), row(R_L2), AX.X, negate=True),
                deps=[e_l2], seg=SG)
    e_ex = P.op("act", lambda e: e.activation(row(R_EXP), row(R_L2), AF.Exp, bias=sc(S_NEGMAX)),
                deps=[e_nm, e_l2], seg=SG)
    e_sm = P.op("dve", lambda e: e.reduce_sum(sc(S_SUM), row(R_EXP), AX.X), deps=[e_ex], seg=SG)
    e_rs = P.op("dve", lambda e: e.reciprocal(sc(S_RSUM), sc(S_SUM)), deps=[e_sm], seg=SG)
    e_pr = P.op("dve", lambda e: e.tensor_scalar_mul(row(R_PR), row(R_EXP), sc(S_RSUM)),
                deps=[e_ex, e_rs], seg=SG)
    e_m1 = P.op("dve", lambda e: e.reduce_max(sc(S_M1), row(R_PR), AX.X), deps=[e_pr], seg=SG)
    e_oh1 = P.op("dve", lambda e: e.tensor_single_scalar(row(R_OH1), row(R_PR), sc(S_M1), ALU.is_ge),
                 deps=[e_m1, e_pr], seg=SG)
    e_t1 = P.op("dve", lambda e: e.tensor_mul(row(R_T1), row(R_OH1), row(R_PR)), deps=[e_oh1], seg=SG)
    e_pm = P.op("dve", lambda e: e.tensor_sub(row(R_PM), row(R_PR), row(R_T1)), deps=[e_t1], seg=SG)
    e_m2_ = P.op("dve", lambda e: e.reduce_max(sc(S_M2), row(R_PM), AX.X), deps=[e_pm], seg=SG)
    e_oh2 = P.op("dve", lambda e: e.tensor_single_scalar(row(R_OH2), row(R_PM), sc(S_M2), ALU.is_ge),
                 deps=[e_m2_, e_pm], seg=SG)
    e_st1 = P.op("pe", lambda e: e.transpose(pv[0:8, 16:17], row(R_OH1), identc[0:1, 0:1]),
                 deps=[e_oh1, e_smm, e_ident], seg=SG)
    e_st2 = P.op("pe", lambda e: e.transpose(pv[0:8, 17:18], row(R_OH2), identc[0:1, 0:1]),
                 deps=[e_oh2], seg=SG)
    e_selT = P.op("dve", lambda e: e.tensor_copy(selT[:], pv[0:8, 16:18]),
                  deps=[e_st1, e_st2], seg=SG)
    e_sb1 = P.op("pe", lambda e: e.matmul(pv[:, 32:40], onesrow[:], row(R_OH1),
                                          start=True, stop=True, skip_group_check=True),
                 deps=[e_oh1, e_or, e_st1, e_selT], seg=SG)
    e_sb2 = P.op("pe", lambda e: e.matmul(pv[:, 40:48], onesrow[:], row(R_OH2),
                                          start=True, stop=True, skip_group_check=True),
                 deps=[e_oh2, e_sb1], seg=SG)
    e_selbc = P.op("dve", lambda e: e.tensor_copy(selbc[:, 0:16], pv[:, 32:48]),
                   deps=[e_sb1, e_sb2], seg=SG)
    e_g1 = P.op("pe", lambda e: e.matmul(pv[:, 48:49], onesrow[:], sc(S_M1),
                                         start=True, stop=True, skip_group_check=True),
                deps=[e_m1, e_sb2, e_selbc], seg=SG)
    e_g2 = P.op("pe", lambda e: e.matmul(pv[:, 49:50], onesrow[:], sc(S_M2),
                                         start=True, stop=True, skip_group_check=True),
                deps=[e_m2_, e_g1], seg=SG)
    e_gbc = P.op("dve", lambda e: e.tensor_copy(gbc[:], pv[:, 48:50]),
                 deps=[e_g1, e_g2], seg=SG)
    e_gath = []
    for k in range(2):
        ek0 = P.op("dve", lambda e, k=k: e.memset(w1fd[:, k * 128:(k + 1) * 128], 0.0), seg=SG)
        ek1 = P.op("dve", lambda e, k=k: e.memset(w1fd[:, (2 + k) * 128:(3 + k) * 128], 0.0), seg=SG)
        last0, last1 = ek0, ek1
        for ei in range(8):
            last0 = P.op("dve", lambda e, k=k, ei=ei: e.scalar_tensor_tensor(
                w1fd[:, k * 128:(k + 1) * 128], w1all[:, ei * 128:(ei + 1) * 128],
                selbc[:, k * 8 + ei:k * 8 + ei + 1], w1fd[:, k * 128:(k + 1) * 128],
                ALU.mult, ALU.add),
                deps=[e_w1all[ei], e_selbc, last0], seg=SG)
            last1 = P.op("dve", lambda e, k=k, ei=ei: e.scalar_tensor_tensor(
                w1fd[:, (2 + k) * 128:(3 + k) * 128], w2all[:, ei * 128:(ei + 1) * 128],
                selbc[:, k * 8 + ei:k * 8 + ei + 1], w1fd[:, (2 + k) * 128:(3 + k) * 128],
                ALU.mult, ALU.add),
                deps=[e_w2all[ei], e_selbc, last1], seg=SG)
        e_gath.append((last0, last1))
    e_wfd_parts = [P.dma("act", lambda e, c=c: e.dma_start(
        out=d_wfd[c * 128:(c + 1) * 128, :],
        in_=w1fd[:, c * 128:(c + 1) * 128]),
        deps=[e_gath[0][0], e_gath[0][1], e_gath[1][0], e_gath[1][1]], seg=SG)
        for c in range(4)]
    e_wfd_st = e_wfd_parts[-1]
    e_wT = []
    for c in range(4):
        e_wT.append(P.dma("act", lambda e, c=c: e.dma_start(
            out=wexp[:, c * 128:(c + 1) * 128],
            in_=d_wfd[c * 128:(c + 1) * 128, :], transpose=True),
            deps=[e_wfd_st], seg=SG))
    e_w2s = []
    for k in range(2):
        e_w2s.append(P.op("dve", lambda e, k=k: e.tensor_scalar_mul(
            wexp[:, (2 + k) * 128:(3 + k) * 128], wexp[:, (2 + k) * 128:(3 + k) * 128],
            gbc[:, k:k + 1]), deps=[e_wT[2 + k], e_gbc], seg=SG))
    e_bg1 = P.op("pe", lambda e: e.matmul(pv[:, 52:54], b1sb[:], selT[:],
                                          start=True, stop=True, skip_group_check=True),
                 deps=[e_selT, e_b1, e_g2, e_gbc], seg=SG)
    e_bg2 = P.op("pe", lambda e: e.matmul(pv[:, 54:56], b2sb[:], selT[:],
                                          start=True, stop=True, skip_group_check=True),
                 deps=[e_selT, e_b2, e_bg1], seg=SG)
    e_b1c = P.op("dve", lambda e: e.tensor_copy(b1cols[:], pv[:, 52:54]), deps=[e_bg1], seg=SG)
    e_b2c = P.op("dve", lambda e: e.tensor_copy(b2cols[:], pv[:, 54:56]), deps=[e_bg2], seg=SG)
    e_bt = P.op("dve", lambda e: e.tensor_scalar_mul(col(C_RA), b2cols[:, 0:1], gbc[:, 0:1]),
                deps=[e_b2c, e_gbc], seg=SG)
    e_bc2 = P.op("dve", lambda e: e.scalar_tensor_tensor(
        bc2col[:], b2cols[:, 1:2], gbc[:, 1:2], col(C_RA), ALU.mult, ALU.add),
        deps=[e_bt], seg=SG)


    uslot = x1b2[:, 0:T]
    x1T3d = bufB[:].rearrange("p (dc t) -> p dc t", dc=NCH)
    e_xw3 = {}
    e_hmm = {}
    e_gel = {}
    e_omm = {}
    e_u1 = {}
    e_u = {}
    e_ymm = {}
    e_z2 = {}
    e_sq2 = {}
    e_outw = {}
    e_ost = {}

    for ts in range(NTS):
        e_xw3[ts] = P.dma("sp", lambda e, ts=ts: e.dma_start(
            out=xw[:], in_=d_x1f[ts * 128:(ts + 1) * 128, :]),
            deps=[e_x1st[NTS - 1] if ts == 0 else None,
                  e_zev[(NTS - 1, 3)] if ts == 0 else None,
                  e_z2.get((ts - 1, 3))], seg=SG)
        for k in range(2):
            for h in range(NCH):
                e = P.op("pe", lambda e, k=k, h=h, ts=ts: e.matmul(
                    pv[:, h * 128:(h + 1) * 128],
                    wexp[:, k * 128:(k + 1) * 128],
                    bufB[:, h * T + ts * 128: h * T + (ts + 1) * 128],
                    start=True, stop=True, skip_group_check=True),
                    deps=[e_wT[k], e_x1T[h],
                          (e_bg2 if ts == 0 and k == 0 else None) if h == 0 else None,
                          (e_b2c if ts == 0 and k == 0 else None) if h == 0 else None,
                          (e_gel[(ts, 0)] if k == 1 else None) if h == 0 else None,
                          e_z2.get((ts - 1, h // 4)) if k == 0 else None],
                    seg=SG)
            e_hmm[(ts, k)] = e
            e_gel[(ts, k)] = P.op("act", lambda e, k=k: e.activation(
                qh[:, 0:T], pv[:], AF.Gelu, bias=b1cols[:, k:k + 1]),
                deps=[e_hmm[(ts, k)], e_b1c,
                      e_omm.get((ts, k - 1)),
                      e_u.get(ts - 1) if k == 0 else None],
                seg=SG)
            for h in range(NCH):
                e = P.op("pe", lambda e, k=k, h=h: e.matmul(
                    pk[:, h * 128:(h + 1) * 128],
                    wexp[:, (2 + k) * 128:(3 + k) * 128],
                    qh[:, h * 128:(h + 1) * 128],
                    start=(k == 0 and h % 4 == 0),
                    stop=(k == 1 and h % 4 == 3), skip_group_check=True),
                    deps=[e_gel[(ts, k)], e_w2s[k],
                          e_sq2.get(ts - 1) if k == 0 and h == 0 else None],
                    seg=SG)
            e_omm[(ts, k)] = e
        e_u1[ts] = P.op("dve", lambda e: e.tensor_scalar_add(qh[:, 0:T], pk[:], bc2col[:]),
                        deps=[e_omm[(ts, 1)], e_bc2, e_gel[(ts, 1)]], seg=SG)
        e_u[ts] = P.op("dve", lambda e, ts=ts: e.tensor_add(
            uslot[:], qh[:, 0:T], x1T3d[:, :, ts * 128:(ts + 1) * 128]),
            deps=[e_u1[ts], e_ymm.get((ts - 1, 3)),
                  e_x1bst[NTS - 2] if ts == 0 else None] +
                 ([d for d in e_x1T] if ts == 0 else []), seg=SG)
        e_ylast = {}
        for dc in range(NCH):
            for fb in range(NTB):
                e_ylast[fb] = P.op("pe", lambda e, fb=fb, dc=dc: e.matmul(
                    pv[:, fb * 512:(fb + 1) * 512],
                    uslot[:, dc * 128:(dc + 1) * 128],
                    bufA[:, dc * T + fb * 512: dc * T + (fb + 1) * 512],
                    start=(dc == 0), stop=False, skip_group_check=True),
                    deps=[e_u[ts] if dc == 0 else None,
                          e_w3[dc] if fb == 0 else None,
                          (e_gel[(ts, 1)] if dc == 0 else None),
                          e_z2.get((ts - 1, fb)) if dc == 0 else None], seg=SG)
        for fb in range(NTB):
            e_ymm[(ts, fb)] = P.op("pe", lambda e, fb=fb: e.matmul(
                pv[:, fb * 512:(fb + 1) * 512], onesb3[:],
                b3row[0:1, fb * 512:(fb + 1) * 512],
                start=False, stop=True, skip_group_check=True),
                deps=[e_ylast[fb], e_b3, e_ob3], seg=SG)
        for fb in range(NTB):
            e_z2[(ts, fb)] = P.op("dve", lambda e, fb=fb: e.tensor_add(
                scr[:, fb * 512:(fb + 1) * 512], pv[:, fb * 512:(fb + 1) * 512],
                xw[:, fb * 512:(fb + 1) * 512]),
                deps=[e_ymm[(ts, fb)], e_xw3[ts],
                      e_lgr if ts == 0 and fb == 0 else None], seg=SG)
        e_ms2 = P.op("dve", lambda e: e.reduce_sum(col(C_MS), scr[:], AX.X),
                     deps=[e_z2[(ts, fb)] for fb in range(NTB)], seg=SG)
        e_mm2 = P.op("dve", lambda e: e.tensor_scalar_mul(col(C_M), col(C_MS), col(C_1OD)),
                     deps=[e_ms2], seg=SG)
        e_sq2[ts] = P.op("act", lambda e: e.activation(
            pk[:], scr[:], AF.Square, accum_out=col(C_S2)),
            deps=[e_z2[(ts, fb)] for fb in range(NTB)] + [e_u1[ts]], seg=SG)
        e_s2m2 = P.op("dve", lambda e: e.tensor_scalar_mul(col(C_S2M), col(C_S2), col(C_1OD)),
                      deps=[e_sq2[ts]], seg=SG)
        e_mq2 = P.op("dve", lambda e: e.tensor_mul(col(C_M2), col(C_M), col(C_M)),
                     deps=[e_mm2], seg=SG)
        e_var2 = P.op("dve", lambda e: e.tensor_sub(col(C_VAR), col(C_S2M), col(C_M2)),
                      deps=[e_s2m2, e_mq2], seg=SG)
        I32 = mybir.dt.int32
        e_n = P.op("dve", lambda e: e.tensor_scalar_add(col(C_WE), col(C_VAR), col(C_EPS)),
                   deps=[e_var2], seg=SG)
        e_n = P.op("dve", lambda e: e.tensor_tensor(
            col(C_T).bitcast(I32), col(C_WE).bitcast(I32), col(C_SH1).bitcast(I32),
            ALU.arith_shift_right), deps=[e_n], seg=SG)
        e_n = P.op("dve", lambda e: e.tensor_tensor(
            col(C_Y).bitcast(I32), col(C_MAGIC).bitcast(I32), col(C_T).bitcast(I32),
            ALU.subtract), deps=[e_n], seg=SG)
        for _ in range(2):
            e_n = P.op("dve", lambda e: e.tensor_mul(col(C_T), col(C_Y), col(C_Y)),
                       deps=[e_n], seg=SG)
            e_n = P.op("dve", lambda e: e.tensor_mul(col(C_T), col(C_T), col(C_WE)),
                       deps=[e_n], seg=SG)
            e_n = P.op("dve", lambda e: e.tensor_scalar(
                col(C_T), col(C_T), col(C_NH), col(C_15), ALU.mult, ALU.add),
                deps=[e_n], seg=SG)
            e_n = P.op("dve", lambda e: e.tensor_mul(col(C_Y), col(C_Y), col(C_T)),
                       deps=[e_n], seg=SG)
        e_rstd2 = e_n
        e_outw[ts] = P.op("dve", lambda e: e.tensor_scalar(
            x1t[:], scr[:], col(C_M), col(C_Y), ALU.subtract, ALU.mult),
            deps=[e_rstd2, e_mm2, e_ost.get(ts - 1), e_x1st[NTS - 1] if ts == 0 else None],
            seg=SG)
        e_ost[ts] = P.dma("sp", lambda e, ts=ts: e.dma_start(
            out=d_out[ts * 128:(ts + 1) * 128, :], in_=x1t[:]),
            deps=[e_outw[ts]], seg=SG)

    P.op("sp", lambda e: e.nop(), deps=[e_ost[NTS - 1]], seg=SG)

    P.emit(nc, block, sems)

    for h in reversed(ctx):
        h.__exit__(None, None, None)
    return nc


_NC_CACHE = None


def kernel(L0_attn_x, L0_attn_Wq, L0_attn_Wk, L0_attn_Wv, L0_attn_Wo,
           L0_ffn_switch_w, L0_ffn_switch_b, L0_ffn_w1, L0_ffn_b1,
           L0_ffn_w2, L0_ffn_b2, L0_ffn_w3, L0_ffn_b3):
    global _NC_CACHE
    f32 = np.float32
    x = np.ascontiguousarray(np.asarray(L0_attn_x, dtype=f32)).reshape(NCORES, T, D)
    xfeat = np.ascontiguousarray(np.swapaxes(x, 1, 2))
    def slab(w):
        w4 = np.asarray(w, dtype=f32).reshape(16, 128, 16, 128)
        return np.ascontiguousarray(w4.transpose(0, 3, 2, 1).reshape(2048, 2048))
    wqT = slab(L0_attn_Wq)
    wkT = slab(L0_attn_Wk)
    wvT = slab(L0_attn_Wv)
    woT = np.ascontiguousarray(np.asarray(L0_attn_Wo, dtype=f32).T)
    w3T = np.ascontiguousarray(np.asarray(L0_ffn_w3, dtype=f32).T)
    sw = np.ascontiguousarray(np.asarray(L0_ffn_switch_w, dtype=f32))
    swb = np.asarray(L0_ffn_switch_b, dtype=f32).reshape(1, 8)
    w1 = np.ascontiguousarray(np.asarray(L0_ffn_w1, dtype=f32).reshape(8 * 128, 128))
    w2 = np.ascontiguousarray(np.asarray(L0_ffn_w2, dtype=f32).reshape(8 * 128, 128))
    b1 = np.ascontiguousarray(np.asarray(L0_ffn_b1, dtype=f32))
    b2 = np.ascontiguousarray(np.asarray(L0_ffn_b2, dtype=f32))
    b3 = np.asarray(L0_ffn_b3, dtype=f32).reshape(1, D)

    if _NC_CACHE is None:
        _NC_CACHE = build_nc()
    nc = _NC_CACHE

    in_maps = []
    for c in range(NCORES):
        sel1 = np.zeros((8, 1), dtype=f32)
        if c % 2 == 1:
            sel1[c - 1, 0] = 1.0
        sel2 = np.zeros((8, 1), dtype=f32)
        sel2[(c // 2) * 2, 0] = 1.0
        in_maps.append({
            "xtok": x[c], "xfeat": xfeat[c],
            "wqT": wqT, "wkT": wkT, "wvT": wvT, "woT": woT, "w3T": w3T,
            "sw": sw, "swb": swb, "w1": w1, "w2": w2, "b1": b1, "b2": b2,
            "b3": b3, "sel1": sel1, "sel2": sel2,
        })
    res = run_bass_kernel_spmd(nc, in_maps, core_ids=list(range(NCORES)))
    out = np.stack([np.asarray(res.results[c]["out"]) for c in range(NCORES)], axis=0)
    return out.reshape(4, 4096, D).astype(np.float32)
